# version 18
# baseline (speedup 1.0000x reference)
import numpy as np
import ml_dtypes
import concourse.bass as bass
import concourse.mybir as mybir
from concourse.alu_op_type import AluOpType
from concourse.bass_utils import run_bass_kernel_spmd
from contextlib import ExitStack

F32 = mybir.dt.float32
BF16 = mybir.dt.bfloat16
I32 = mybir.dt.int32
AF = mybir.ActivationFunctionType
AX = mybir.AxisListType
OP = AluOpType

D = 2048
S = 2048
HEADS = 16
GROUPS = 4
KVC = 128
HID = 5632
HC = HID // 128
NCORES = 8
SEQPC = 2
TILE = 512
NT_SEQ = S // TILE
SCALE = KVC ** -0.5
EPS = 1e-5
NEGBIG = -1.0e30


class Sem:
    def __init__(self, stack, nc, name):
        self.h = stack.enter_context(nc.semaphore(name))
        self.n = 0

    def inc(self, instr, k=1):
        instr.then_inc(self.h, k)
        self.n += k
        return self.n


class Waiter:
    def __init__(self):
        self.last = {}

    def wait(self, eng, sem, val):
        if val <= 0:
            return
        key = (id(eng), id(sem))
        if self.last.get(key, 0) >= val:
            return
        eng.wait_ge(sem.h, val)
        self.last[key] = val


def build_program(n_tiles=SEQPC * NT_SEQ, dbg=None):
    nc = bass.Bass("TRN2", target_bir_lowering=False)
    NROW = SEQPC * S

    x_in = nc.dram_tensor("x_in", [NROW, D], F32, kind="ExternalInput")
    wq_f = nc.dram_tensor("wq_f", [D, D], F32, kind="ExternalInput")
    wkv_f = nc.dram_tensor("wkv_f", [D, 2 * GROUPS * KVC], F32, kind="ExternalInput")
    wd_f = nc.dram_tensor("wd_f", [D, D], F32, kind="ExternalInput")
    wg_i = nc.dram_tensor("wg_i", [HID, D], I32, kind="ExternalInput")
    wu_i = nc.dram_tensor("wu_i", [HID, D], I32, kind="ExternalInput")
    wn_i = nc.dram_tensor("wn_i", [HID, D], I32, kind="ExternalInput")
    sg_in = nc.dram_tensor("sg_in", [128, HC], F32, kind="ExternalInput")
    su_in = nc.dram_tensor("su_in", [128, HC], F32, kind="ExternalInput")
    sdb_in = nc.dram_tensor("sdb_in", [128, D], F32, kind="ExternalInput")
    mask_in = nc.dram_tensor("mask_in", [128, 128], BF16, kind="ExternalInput")
    id_in = nc.dram_tensor("id_in", [128, 128], BF16, kind="ExternalInput")
    out = nc.dram_tensor("out", [NROW, D], F32, kind="ExternalOutput")

    wq_b = nc.dram_tensor("wq_b", [D, D], BF16, kind="Internal")
    wkv_b = nc.dram_tensor("wkv_b", [D, 2 * GROUPS * KVC], BF16, kind="Internal")
    wd_b = nc.dram_tensor("wd_b", [D, D], BF16, kind="Internal")
    wg_b = nc.dram_tensor("wg_b", [HID, D], BF16, kind="Internal")
    wu_b = nc.dram_tensor("wu_b", [HID, D], BF16, kind="Internal")
    wn_b = nc.dram_tensor("wn_b", [4, HC // 4, 128, 2048], BF16, kind="Internal")

    PE, DV, AC, SY, GP = nc.tensor, nc.vector, nc.scalar, nc.sync, nc.gpsimd
    W = Waiter()
    st = ExitStack()

    def mks(name):
        return Sem(st, nc, name)
    s_pe, s_dv, s_ac = mks("s_pe"), mks("s_dv"), mks("s_ac")
    s_cst = mks("s_cst")
    s_p0a = mks("s_p0a")
    s_cv = [mks(f"s_cv{i}") for i in range(2)]
    s_cs = [mks(f"s_cs{i}") for i in range(2)]
    s_pg = mks("s_pg")
    s_pu = mks("s_pu")
    s_pp = mks("s_pp")
    s_wq = [mks(f"s_wq{i}") for i in range(2)]
    s_wkv = [mks(f"s_wkv{i}") for i in range(2)]
    s_wd = [mks(f"s_wd{i}") for i in range(2)]
    s_wg = [mks(f"s_wg{i}") for i in range(2)]
    s_wu = [mks(f"s_wu{i}") for i in range(2)]
    s_wn = [mks(f"s_wn{i}") for i in range(2)]
    s_xr = [mks(f"s_xr{i}") for i in range(2)]
    s_xb = [mks(f"s_xb{i}") for i in range(3)]
    s_xc = mks("s_xc")
    s_stw = mks("s_stw")
    s_out = [mks(f"s_out{i}") for i in range(2)]

    def sb(name, shape, dt):
        return st.enter_context(nc.sbuf_tensor(name, shape, dt))

    c_mask = sb("c_mask", [128, 128], BF16)
    c_id = sb("c_id", [128, 128], BF16)
    c_sg = sb("c_sg", [128, HC], F32)
    c_su = sb("c_su", [128, HC], F32)
    K_fm = sb("K_fm", [128, GROUPS * S], BF16)
    V_tm = sb("V_tm", [128, GROUPS * 16 * 128], BF16)
    Q_bf = sb("Q_bf", [128, HEADS * TILE], BF16)
    act_fm = sb("act_fm", [128, 16 * TILE], BF16)
    xbf = [sb(f"xbf{i}", [128, D], BF16) for i in range(3)]
    xres = [sb(f"xres{i}", [128, D], F32) for i in range(2)]
    x1b = sb("x1b", [128, 4 * D], BF16)
    gu = sb("gu", [128, HC * TILE], BF16)
    gsil = [sb(f"gsil{i}", [128, TILE], BF16) for i in range(2)]
    sdb_bf = sb("sdb_bf", [128, D], BF16)
    sums = sb("sums", [128, HEADS], F32)
    invs = sb("invs", [128, HEADS], F32)
    bnb = sb("bnb", [128, 24], F32)
    mv01 = sb("mv01", [128, 2], F32)
    negm = sb("negm", [128, 1], F32)
    lntmp = sb("lntmp", [128, 1], F32)
    rstd = sb("rstd", [128, 1], F32)
    epsv = sb("epsv", [128, 1], F32)
    shiftv = sb("shiftv", [128, 1], F32)
    wqb = [sb(f"wqb{i}", [128, 1024], BF16) for i in range(2)]
    wkvb = [sb(f"wkvb{i}", [128, 1024], BF16) for i in range(2)]
    wdb = [sb(f"wdb{i}", [128, 2048], BF16) for i in range(2)]
    wnb = [sb(f"wnb{i}", [128, D], BF16) for i in range(2)]
    m128 = sb("m128", [128, D], BF16)

    s_cst.inc(SY.dma_start(c_mask[:, :], mask_in[:, :]), 16)
    s_cst.inc(SY.dma_start(c_id[:, :], id_in[:, :]), 16)
    s_cst.inc(SY.dma_start(c_sg[:, :], sg_in[:, :]), 16)
    s_cst.inc(SY.dma_start(c_su[:, :], su_in[:, :]), 16)
    CST_ALL = s_cst.n
    s_dv.inc(DV.memset(epsv[:, :], EPS))
    s_dv.inc(DV.memset(shiftv[:, :], -16.0))
    m128_tok = s_dv.inc(DV.memset(m128[:, :], -128.0))

    for i in range(4):
        s_p0a.inc(GP.dma_start(wq_b[i * 512:(i + 1) * 512, :],
                               wq_f[i * 512:(i + 1) * 512, :]), 16)
    P0_WQ = s_p0a.n
    for i in range(4):
        s_p0a.inc(GP.dma_start(wkv_b[i * 512:(i + 1) * 512, :],
                               wkv_f[i * 512:(i + 1) * 512, :]), 16)
    P0_WKV = s_p0a.n
    for i in range(4):
        s_p0a.inc(GP.dma_start(wd_b[i * 512:(i + 1) * 512, :],
                               wd_f[i * 512:(i + 1) * 512, :]), 16)
    P0_WD = s_p0a.n

    xbf_need = []

    def load_xbf(gchunk):
        trow0 = (gchunk // 4) * TILE + (gchunk % 4) * 128
        slot = gchunk % 3
        if gchunk >= 3:
            W.wait(GP, s_xc, gchunk - 2)
        GP.dma_start(xbf[slot][:, :], x_in[trow0: trow0 + 128, :]
                     ).then_inc(s_xb[slot].h, 16)
        s_xb[slot].n += 16
        xbf_need.append((slot, s_xb[slot].n))

    for _c in range(3):
        load_xbf(_c)

    s_cst.inc(GP.dma_start(sdb_bf[:, :], sdb_in[:, :]), 16)
    CST_SDB = s_cst.n
    n_chunks = n_tiles * 4
    for _c in range(3, min(6, n_chunks)):
        load_xbf(_c)
    wg_store_val = {}
    wu_store_val = {}
    wn_store_val = {}
    stw_pend_g = [0, 0]; stw_pend_u = [0, 0]; stw_pend_n = [0, 0]
    next_xbf = [min(6, n_chunks)]

    wq_ld = [0]; wkv_ld = [0]; wd_ld = [0]; wn_ld = [0]; xr_ld = [0]
    wq_cons = [0, 0]; wkv_cons = [0, 0]; wd_cons = [0, 0]; wn_cons = [0, 0]
    xr_cons = [0, 0]
    out_st = [0, 0]
    psum_id = [0]

    def psname(tag):
        psum_id[0] += 1
        return f"ps_{tag}_{psum_id[0]}"

    def stream_load(sem_list, cons, ld_cnt, buf_list, src_ap_fn, gate_extra=None):
        sl = ld_cnt[0] % 2
        W.wait(SY, s_pe, cons[sl])
        if gate_extra is not None:
            gate_extra(sl)
        SY.dma_start(buf_list[sl][:, :], src_ap_fn()).then_inc(sem_list[sl].h, 16)
        sem_list[sl].n += 16
        ld_cnt[0] += 1
        return sl, sem_list[sl].n

    dbg = dbg or []
    dbg_bufs = {"K_fm": K_fm, "V_tm": V_tm, "Q_bf": Q_bf, "act_fm": act_fm,
                "x1b": x1b, "gu": gu}
    ln2_stt_last = [0]
    last_ctx_pe = [0]
    ln_apply_tok = [0]
    prev_dv_end = [0]

    for t in range(n_tiles):
        tt = t % NT_SEQ
        trow = t * TILE

        while next_xbf[0] < min(4 * t + 6, n_chunks):
            load_xbf(next_xbf[0])
            next_xbf[0] += 1
        with ExitStack() as ps:
            ps.enter_context(nc.named_scope(f"xT{t}", notify=True))
            pxt = [ps.enter_context(nc.psum_tensor(psname("xt"), [128, 1024], BF16))
                   for _ in range(2)]
            W.wait(PE, s_dv, prev_dv_end[0])
            evac_tok = {}
            for c in range(4):
                slot, semval = xbf_need[t * 4 + c]
                W.wait(PE, s_xb[slot], semval)
                if t == 0 and c == 0:
                    W.wait(PE, s_cst, CST_ALL)
                for g4 in range(4):
                    bank = (c * 4 + g4) % 2
                    if (c * 4 + g4) >= 2:
                        W.wait(PE, s_dv, evac_tok.get(c * 4 + g4 - 2, 0))
                    for j in range(4):
                        hc = g4 * 4 + j
                        ins = PE.transpose(pxt[bank][:, j * 128:(j + 1) * 128],
                                           xbf[slot][:, hc * 128:(hc + 1) * 128],
                                           c_id[:, :])
                    pe_tok = s_pe.inc(ins)
                    W.wait(DV, s_pe, pe_tok)
                    dst = act_fm[:, :].rearrange("p (hc t) -> p hc t", hc=16)[
                        :, g4 * 4:(g4 + 1) * 4, c * 128:(c + 1) * 128]
                    evac_tok[c * 4 + g4] = s_dv.inc(DV.tensor_copy(dst, pxt[bank][:, 0:512]))
                nc.tensor.nop().then_inc(s_xc.h, 1)
                s_xc.n += 1
            xfm_tok = s_dv.n

        with ExitStack() as ps:
            ps.enter_context(nc.named_scope(f"qkv{t}", notify=True))
            pq = [ps.enter_context(nc.psum_tensor(psname("q"), [128, 512], F32))
                  for _ in range(8)]
            W.wait(PE, s_dv, xfm_tok)
            qev_tok = 0
            for half in range(2):
                if half == 1:
                    W.wait(PE, s_ac, qev_tok)
                for hc in range(16):
                    def gate(sl):
                        if t == 0 and wq_ld[0] < 2:
                            W.wait(SY, s_p0a, P0_WQ)
                    sl, need = stream_load(
                        s_wq, wq_cons, wq_ld, wqb,
                        lambda: wq_b[hc * 128:(hc + 1) * 128,
                                     half * 1024:(half + 1) * 1024], gate)
                    W.wait(PE, s_wq[sl], need)
                    for ec in range(8):
                        ins = PE.matmul(pq[ec][:, :],
                                        wqb[sl][:, ec * 128:(ec + 1) * 128],
                                        act_fm[:, hc * 512:(hc + 1) * 512],
                                        start=(hc == 0), stop=(hc == 15))
                    wq_cons[sl] = s_pe.inc(ins)
                mm_tok = s_pe.n
                W.wait(AC, s_pe, mm_tok)
                for ec in range(8):
                    h_abs = half * 8 + ec
                    ins = AC.activation(
                        Q_bf[:, :].rearrange("p (h t) -> p h t", h=16)[:, h_abs, :],
                        pq[ec][:, :], AF.Copy, bias=0.0, scale=SCALE)
                qev_tok = s_ac.inc(ins)
            W.wait(PE, s_ac, qev_tok)
            for hc in range(16):
                def gate(sl):
                    if t == 0 and wkv_ld[0] < 2:
                        W.wait(SY, s_p0a, P0_WKV)
                sl, need = stream_load(
                    s_wkv, wkv_cons, wkv_ld, wkvb,
                    lambda: wkv_b[hc * 128:(hc + 1) * 128, :], gate)
                W.wait(PE, s_wkv[sl], need)
                for dc in range(4):
                    ins = PE.matmul(pq[dc][:, :], wkvb[sl][:, dc * 128:(dc + 1) * 128],
                                    act_fm[:, hc * 512:(hc + 1) * 512],
                                    start=(hc == 0), stop=(hc == 15))
                for tc in range(4):
                    ins = PE.matmul(pq[4 + tc][:, :],
                                    act_fm[:, hc * 512 + tc * 128: hc * 512 + (tc + 1) * 128],
                                    wkvb[sl][:, 512:1024],
                                    start=(hc == 0), stop=(hc == 15))
                wkv_cons[sl] = s_pe.inc(ins)
            kv_tok = s_pe.n
            W.wait(DV, s_pe, kv_tok)
            for dc in range(4):
                ins = DV.tensor_copy(K_fm[:, dc * S + tt * 512: dc * S + (tt + 1) * 512],
                                     pq[dc][:, :])
            for tc in range(4):
                kc = tt * 4 + tc
                dst = V_tm[:, :].rearrange("p (g kc d) -> p g kc d", g=4, kc=16)[
                    :, :, kc, :]
                ins = DV.tensor_copy(dst, pq[4 + tc][:, :].rearrange(
                    "p (g d) -> p g d", g=4))
            kvev_tok = s_dv.inc(ins)

        stt_prev = [0]
        for qp in range(2):
            with ExitStack() as pairsc:
                pairsc.enter_context(nc.named_scope(f"attn{t}_{qp}", notify=True))
                ctx2 = [pairsc.enter_context(
                    nc.sbuf_tensor(psname("ctx"), [128, 2048], BF16)) for _ in range(2)]
                ctx_done = 0
                exp_all = 0
                for qq in range(2):
                    qc = qp * 2 + qq
                    qcg = tt * 4 + qc
                    klen = (qcg + 1) * 128
                    nkc = qcg + 1
                    ctx = ctx2[qq]
                    with ExitStack() as qs:
                        probs = [qs.enter_context(nc.sbuf_tensor(
                            psname("probs"), [128, 2048], BF16)) for _ in range(2)]
                        probsT = qs.enter_context(nc.sbuf_tensor(
                            psname("probsT"), [128, 2048], BF16))
                        ps_s = qs.enter_context(nc.psum_tensor(psname("s"), [128, 2048], F32))
                        ps_pt = [qs.enter_context(nc.psum_tensor(
                            psname("pt"), [128, 1024], BF16)) for _ in range(2)]
                        ps_cx = [qs.enter_context(nc.psum_tensor(
                            psname("cx"), [128, 512], F32)) for _ in range(2)]

                        W.wait(PE, s_dv, kvev_tok)
                        W.wait(PE, s_dv, stt_prev[0])
                        W.wait(PE, s_ac, s_ac.n)
                        exp_tok = {}; ptev_tok = {}; ctx_tok = {}; ctxev_tok = {}
                        norm_tok = {}
                        for it in range(HEADS + 1):
                            if it < HEADS:
                                h = it
                                g = h // 4
                                if h > 0:
                                    W.wait(PE, s_ac, exp_tok[h - 1])
                                lhsQ = Q_bf[:, h * 512 + qc * 128: h * 512 + (qc + 1) * 128]
                                base = klen - 128
                                blo = (base // 512) * 512
                                if base % 512 != 0:
                                    PE.matmul(ps_s[:, blo:base], lhsQ,
                                              K_fm[:, g * S + blo: g * S + base],
                                              start=True, stop=True)
                                ins = PE.matmul(ps_s[:, base:klen], lhsQ,
                                                K_fm[:, g * S + base: g * S + klen],
                                                start=True, stop=True)
                                scd_tok = s_pe.inc(ins)
                                for j in range(blo // 512):
                                    ins = PE.matmul(
                                        ps_s[:, j * 512:(j + 1) * 512], lhsQ,
                                        K_fm[:, g * S + j * 512: g * S + (j + 1) * 512],
                                        start=True, stop=True)
                                sc_tok = s_pe.inc(ins) if blo > 0 else scd_tok
                                W.wait(DV, s_pe, scd_tok)
                                mtok = s_dv.inc(DV.tensor_tensor(
                                    ps_s[:, klen - 128:klen], ps_s[:, klen - 128:klen],
                                    c_mask[:, :], OP.add))
                                sc_mm = (sc_tok, mtok)
                            if it >= 1:
                                h = it - 1
                                g = h // 4
                                W.wait(PE, s_dv, norm_tok[h])
                                nbat = (nkc + 3) // 4
                                for b in range(nbat):
                                    if b >= 2:
                                        W.wait(PE, s_dv, ptev_tok[(h, b - 2)])
                                    kcs = list(range(b * 4, min(nkc, b * 4 + 4)))
                                    for kk in kcs:
                                        ins = PE.transpose(
                                            ps_pt[b % 2][:, (kk % 4) * 128:(kk % 4 + 1) * 128],
                                            probs[h % 2][:, kk * 128:(kk + 1) * 128],
                                            c_id[:, :])
                                    ptok = s_pe.inc(ins)
                                    W.wait(DV, s_pe, ptok)
                                    if b == 0:
                                        W.wait(DV, s_pe, ctx_tok[h - 1] if h > 0
                                               else last_ctx_pe[0])
                                    wdt = 128 * len(kcs)
                                    ptev_tok[(h, b)] = s_dv.inc(DV.tensor_copy(
                                        probsT[:, b * 512: b * 512 + wdt],
                                        ps_pt[b % 2][:, 0:wdt]))
                                if h % 4 == 0:
                                    W.wait(PE, s_dv, ctxev_tok.get(g - 2, 0))
                                for kk in range(nkc):
                                    W.wait(PE, s_dv, ptev_tok[(h, kk // 4)])
                                    ins = PE.matmul(
                                        ps_cx[g % 2][:, (h % 4) * 128:(h % 4 + 1) * 128],
                                        V_tm[:, (g * 16 + kk) * 128:(g * 16 + kk + 1) * 128],
                                        probsT[:, kk * 128:(kk + 1) * 128],
                                        start=(kk == 0), stop=(kk == nkc - 1))
                                ctx_tok[h] = s_pe.inc(ins)
                                last_ctx_pe[0] = ctx_tok[h]
                                if h % 4 == 3:
                                    W.wait(DV, s_pe, ctx_tok[h])
                                    ctxev_tok[g] = s_dv.inc(DV.tensor_copy(
                                        ctx[:, g * 512:(g + 1) * 512], ps_cx[g % 2][:, :]))
                            if it < HEADS:
                                h = it
                                sc_tok, mtok = sc_mm
                                W.wait(AC, s_dv, mtok)
                                W.wait(AC, s_pe, sc_tok)
                                exp_tok[h] = s_ac.inc(AC.activation(
                                    probs[h % 2][:, 0:klen], ps_s[:, 0:klen], AF.Exp,
                                    bias=shiftv[:, 0:1], scale=1.0,
                                    accum_out=sums[:, h:h + 1]))
                                W.wait(DV, s_ac, exp_tok[h])
                                rc_tok = s_dv.inc(DV.reciprocal(invs[:, h:h + 1],
                                                                sums[:, h:h + 1]))
                                W.wait(DV, s_dv, rc_tok)
                                norm_tok[h] = s_dv.inc(DV.tensor_scalar(
                                    probs[h % 2][:, 0:klen], probs[h % 2][:, 0:klen],
                                    invs[:, h:h + 1], None, OP.mult))
                        ctx_done = s_dv.n
                        exp_all = s_ac.n
                        if dbg and "taps" in dbg and t == 0:
                            dt_ = nc.dram_tensor(f"dbg_ctx_{qc}", [128, 2048], BF16,
                                                 kind="ExternalOutput")
                            dt2_ = nc.dram_tensor(f"dbg_probsT_{qc}", [128, 2048], BF16,
                                                  kind="ExternalOutput")
                            dt3_ = nc.dram_tensor(f"dbg_sums_{qc}", [128, HEADS], F32,
                                                  kind="ExternalOutput")
                            W.wait(SY, s_dv, ctx_done)
                            SY.dma_start(dt_[:, :], ctx[:, :]).then_inc(s_cst.h, 16)
                            SY.dma_start(dt2_[:, :], probsT[:, :]).then_inc(s_cst.h, 16)
                            SY.dma_start(dt3_[:, :], sums[:, :]).then_inc(s_cst.h, 16)
                            s_cst.n += 48
                            W.wait(SY, s_cst, s_cst.n)

                with ExitStack() as ps2:
                    ps2.enter_context(nc.named_scope(f"dense{t}_{qp}", notify=True))
                    ps_do = [ps2.enter_context(nc.psum_tensor(psname("do"), [128, 2048], F32))
                             for _ in range(2)]
                    xr_need = [0, 0]
                    for qq in range(2):
                        qc = qp * 2 + qq
                        sl = qc % 2
                        W.wait(SY, s_dv, xr_cons[sl])
                        W.wait(SY, s_out[sl], out_st[sl])
                        SY.dma_start(xres[sl][:, :],
                                     x_in[trow + qc * 128: trow + (qc + 1) * 128, :]
                                     ).then_inc(s_xr[sl].h, 16)
                        s_xr[sl].n += 16
                        xr_need[qq] = s_xr[sl].n
                        xr_ld[0] += 1
                    W.wait(PE, s_dv, ctx_done)
                    W.wait(PE, s_ac, exp_all)
                    for dc in range(16):
                        def gate(wsl):
                            if t == 0 and wd_ld[0] < 2:
                                W.wait(SY, s_p0a, P0_WD)
                        wsl, need = stream_load(
                            s_wd, wd_cons, wd_ld, wdb,
                            lambda: wd_b[dc * 128:(dc + 1) * 128, :], gate)
                        W.wait(PE, s_wd[wsl], need)
                        for qq in range(2):
                            for n in range(4):
                                ins = PE.matmul(
                                    ps_do[qq][:, n * 512:(n + 1) * 512],
                                    ctx2[qq][:, dc * 128:(dc + 1) * 128],
                                    wdb[wsl][:, n * 512:(n + 1) * 512],
                                    start=(dc == 0), stop=(dc == 15))
                        wd_cons[wsl] = s_pe.inc(ins)
                    do_tok = s_pe.n
                    for qq in range(2):
                        qc = qp * 2 + qq
                        sl = qc % 2
                        W.wait(DV, s_pe, do_tok)
                        W.wait(DV, s_xr[sl], xr_need[qq])
                        stt_tok = s_dv.inc(DV.scalar_tensor_tensor(
                            xres[sl][:, :], ps_do[qq][:, :], 1.0, xres[sl][:, :],
                            OP.mult, OP.add))
                        xr_cons[sl] = stt_tok
                        stt_prev[0] = stt_tok
                        if dbg and "taps" in dbg and t == 0:
                            dt_ = nc.dram_tensor(f"dbg_ln1pre_{qc}", [128, D], F32,
                                                 kind="ExternalOutput")
                            W.wait(SY, s_dv, stt_tok)
                            SY.dma_start(dt_[:, :], xres[(qp*2+qq) % 2][:, :]).then_inc(s_cst.h, 16)
                            s_cst.n += 16
                            W.wait(DV, s_cst, s_cst.n)
                        for j in range(4):
                            ins = DV.bn_stats(bnb[:, j * 6:(j + 1) * 6],
                                              xres[sl][:, j * 512:(j + 1) * 512])
                        btok = s_dv.inc(ins)
                        W.wait(DV, s_dv, btok)
                        atok = s_dv.inc(DV.bn_aggr(
                            mv01[:, :], bnb[:, :].rearrange("p (n t) -> p n t", t=3)))
                        W.wait(DV, s_dv, atok)
                        ngtok = s_dv.inc(DV.tensor_scalar(
                            negm[:, :], mv01[:, 0:1], -1.0, None, OP.mult))
                        W.wait(AC, s_dv, atok)
                        lntok = s_ac.inc(AC.activation(
                            lntmp[:, :], mv01[:, 1:2], AF.Ln, bias=epsv[:, 0:1], scale=1.0))
                        W.wait(AC, s_ac, lntok)
                        W.wait(AC, s_dv, ln_apply_tok[0])
                        rstok = s_ac.inc(AC.activation(
                            rstd[:, :], lntmp[:, :], AF.Exp, bias=0.0, scale=-0.5))
                        W.wait(DV, s_ac, rstok)
                        W.wait(DV, s_dv, ngtok)
                        ln_apply_tok[0] = s_dv.inc(DV.tensor_scalar(
                            x1b[:, qc * D:(qc + 1) * D], xres[sl][:, :],
                            negm[:, 0:1], rstd[:, 0:1], OP.add, OP.mult))
                        xr_cons[sl] = ln_apply_tok[0]
        ln1_done = s_dv.n

        with ExitStack() as ps:
            ps.enter_context(nc.named_scope(f"x1T{t}", notify=True))
            pxt = [ps.enter_context(nc.psum_tensor(psname("x1t"), [128, 1024], BF16))
                   for _ in range(2)]
            W.wait(PE, s_dv, ln1_done)
            evac_tok = {}
            for qc in range(4):
                for g4 in range(4):
                    bank = (qc * 4 + g4) % 2
                    if (qc * 4 + g4) >= 2:
                        W.wait(PE, s_dv, evac_tok.get(qc * 4 + g4 - 2, 0))
                    for j in range(4):
                        dc = g4 * 4 + j
                        ins = PE.transpose(pxt[bank][:, j * 128:(j + 1) * 128],
                                           x1b[:, qc * D + dc * 128: qc * D + (dc + 1) * 128],
                                           c_id[:, :])
                    pe_tok = s_pe.inc(ins)
                    W.wait(DV, s_pe, pe_tok)
                    dst = act_fm[:, :].rearrange("p (hc t) -> p hc t", hc=16)[
                        :, g4 * 4:(g4 + 1) * 4, qc * 128:(qc + 1) * 128]
                    evac_tok[qc * 4 + g4] = s_dv.inc(DV.tensor_copy(dst, pxt[bank][:, 0:512]))
            x1fm_tok = s_dv.n

        with ExitStack() as ps:
            ps.enter_context(nc.named_scope(f"gate{t}", notify=True))
            pg = [ps.enter_context(nc.psum_tensor(psname("g"), [128, 512], F32))
                  for _ in range(2)]
            pu = [ps.enter_context(nc.psum_tensor(psname("u"), [128, 512], F32))
                  for _ in range(2)]
            wgu = [ps.enter_context(nc.sbuf_tensor(psname("wgu"), [128, D], BF16))
                   for _ in range(2)]
            wub = [ps.enter_context(nc.sbuf_tensor(psname("wub"), [128, D], BF16))
                   for _ in range(2)]
            wgu_cons = [s_pe.n, s_pe.n]; wub_cons = [s_pe.n, s_pe.n]
            W.wait(SY, s_dv, s_dv.n)
            W.wait(PE, s_dv, x1fm_tok)
            sil_tok = {}; gu_tok = {}
            s_wg_need = {}; s_wu_need = {}
            for hc in range(HC):
                sl = hc % 2
                if t == 0:
                    for (buf, sem, m_i, m_b, sval, spend) in (
                            (wgu, s_wg, wg_i, wg_b, wg_store_val, stw_pend_g),
                            (wub, s_wu, wu_i, wu_b, wu_store_val, stw_pend_u)):
                        W.wait(GP, s_pe, wgu_cons[sl] if buf is wgu else wub_cons[sl])
                        W.wait(GP, s_stw, spend[sl])
                        W.wait(GP, s_dv, m128_tok)
                        GP.dma_start(buf[sl][:, :], m128[:, :]).then_inc(sem[sl].h, 16)
                        sem[sl].n += 16
                        W.wait(GP, sem[sl], sem[sl].n)
                        GP.dma_start(buf[sl][:, :], m_i[hc * 128:(hc + 1) * 128, :],
                                     accum_op=OP.add).then_inc(sem[sl].h, 16)
                        sem[sl].n += 16
                        W.wait(GP, sem[sl], sem[sl].n)
                        GP.dma_start(m_b[hc * 128:(hc + 1) * 128, :], buf[sl][:, :]
                                     ).then_inc(s_stw.h, 16)
                        s_stw.n += 16
                        sval[hc] = s_stw.n
                        spend[sl] = s_stw.n
                    wg_need = s_wg[sl].n
                    wu_need = s_wu[sl].n
                else:
                    W.wait(SY, s_pe, wgu_cons[sl])
                    if t == 1:
                        W.wait(SY, s_stw, wg_store_val[hc])
                    SY.dma_start(wgu[sl][:, :], wg_b[hc * 128:(hc + 1) * 128, :]
                                 ).then_inc(s_wg[sl].h, 16)
                    s_wg[sl].n += 16
                    wg_need = s_wg[sl].n
                    W.wait(SY, s_pe, wub_cons[sl])
                    if t == 1:
                        W.wait(SY, s_stw, wu_store_val[hc])
                    SY.dma_start(wub[sl][:, :], wu_b[hc * 128:(hc + 1) * 128, :]
                                 ).then_inc(s_wu[sl].h, 16)
                    s_wu[sl].n += 16
                    wu_need = s_wu[sl].n
                W.wait(PE, s_wg[sl], wg_need)
                W.wait(PE, s_wu[sl], wu_need)
                if hc >= 2:
                    W.wait(PE, s_ac, sil_tok[hc - 2])
                    W.wait(PE, s_dv, gu_tok[hc - 2])
                for dc in range(16):
                    ins = PE.matmul(pg[sl][:, :], wgu[sl][:, dc * 128:(dc + 1) * 128],
                                    act_fm[:, dc * 512:(dc + 1) * 512],
                                    start=(dc == 0), stop=(dc == 15))
                g_tok = s_pe.inc(ins)
                for dc in range(16):
                    ins = PE.matmul(pu[sl][:, :], wub[sl][:, dc * 128:(dc + 1) * 128],
                                    act_fm[:, dc * 512:(dc + 1) * 512],
                                    start=(dc == 0), stop=(dc == 15))
                u_tok = s_pe.inc(ins)
                wgu_cons[sl] = u_tok
                wub_cons[sl] = u_tok
                W.wait(AC, s_pe, g_tok)
                sil_tok[hc] = s_ac.inc(AC.activation(
                    gsil[sl][:, :], pg[sl][:, :], AF.Silu,
                    bias=0.0, scale=c_sg[:, hc:hc + 1]))
                W.wait(DV, s_pe, u_tok)
                W.wait(DV, s_ac, sil_tok[hc])
                gu_tok[hc] = s_dv.inc(DV.scalar_tensor_tensor(
                    gu[:, hc * 512:(hc + 1) * 512], pu[sl][:, :],
                    c_su[:, hc:hc + 1], gsil[sl][:, :], OP.mult, OP.mult))
            gu_done = s_dv.n

        with ExitStack() as ps:
            ps.enter_context(nc.named_scope(f"down{t}", notify=True))
            pdn = [ps.enter_context(nc.psum_tensor(psname("dn"), [128, 512], F32))
                   for _ in range(4)]
            y2 = ps.enter_context(nc.sbuf_tensor(psname("y2"), [128, 4 * D], BF16))
            bnb4 = ps.enter_context(nc.sbuf_tensor(psname("bnb4"), [128, 96], F32))
            W.wait(PE, s_dv, gu_done)
            ev_tok = {}
            for eq in range(4):
                for hcg in range(11):
                    if t == 0:
                        wsl = wn_ld[0] % 2
                        W.wait(GP, s_pe, wn_cons[wsl])
                        W.wait(GP, s_stw, stw_pend_n[wsl])
                        W.wait(GP, s_dv, m128_tok)
                        GP.dma_start(wnb[wsl][:, :], m128[:, :]
                                     ).then_inc(s_wn[wsl].h, 16)
                        s_wn[wsl].n += 16
                        W.wait(GP, s_wn[wsl], s_wn[wsl].n)
                        GP.dma_start(
                            wnb[wsl][:, :],
                            wn_i[:, :].rearrange("(hc p) e -> p hc e", p=128)[
                                :, hcg * 4:(hcg + 1) * 4,
                                eq * 512:(eq + 1) * 512],
                            accum_op=OP.add).then_inc(s_wn[wsl].h, 16)
                        s_wn[wsl].n += 16
                        need = s_wn[wsl].n
                        W.wait(GP, s_wn[wsl], need)
                        GP.dma_start(wn_b[eq, hcg, :, :], wnb[wsl][:, :]
                                     ).then_inc(s_stw.h, 16)
                        s_stw.n += 16
                        wn_store_val[(eq, hcg)] = s_stw.n
                        stw_pend_n[wsl] = s_stw.n
                        wn_ld[0] += 1
                    else:
                        def gate(wsl):
                            if t == 1:
                                W.wait(SY, s_stw, wn_store_val[(eq, hcg)])
                        wsl, need = stream_load(
                            s_wn, wn_cons, wn_ld, wnb,
                            lambda: wn_b[eq, hcg, :, :], gate)
                    W.wait(PE, s_wn[wsl], need)
                    if hcg == 0 and eq > 0:
                        W.wait(PE, s_dv, ev_tok[eq - 1])
                    for j in range(4):
                        hc = hcg * 4 + j
                        for qc in range(4):
                            ins = PE.matmul(
                                pdn[qc][:, :],
                                gu[:, hc * 512 + qc * 128: hc * 512 + (qc + 1) * 128],
                                wnb[wsl][:, j * 512:(j + 1) * 512],
                                start=(hc == 0), stop=(hc == HC - 1))
                    wn_cons[wsl] = s_pe.inc(ins)
                dn_tok = s_pe.n
                W.wait(DV, s_pe, dn_tok)
                W.wait(DV, s_cst, CST_SDB)
                for qc in range(4):
                    sli = slice(qc * D + eq * 512, qc * D + (eq + 1) * 512)
                    s_dv.inc(DV.tensor_tensor(
                        y2[:, sli], pdn[qc][:, :],
                        sdb_bf[:, eq * 512:(eq + 1) * 512], OP.mult))
                    s_dv.inc(DV.tensor_tensor(y2[:, sli], y2[:, sli],
                                              x1b[:, sli], OP.add))
                    ins = DV.bn_stats(bnb4[:, qc * 24 + eq * 6: qc * 24 + (eq + 1) * 6],
                                      y2[:, sli])
                ev_tok[eq] = s_dv.inc(ins)
            for qc in range(4):
                sl = qc % 2
                W.wait(DV, s_dv, ev_tok[3])
                atok = s_dv.inc(DV.bn_aggr(
                    mv01[:, :],
                    bnb4[:, qc * 24:(qc + 1) * 24].rearrange("p (n t) -> p n t", t=3)))
                W.wait(DV, s_dv, atok)
                ngtok = s_dv.inc(DV.tensor_scalar(
                    negm[:, :], mv01[:, 0:1], -1.0, None, OP.mult))
                W.wait(AC, s_dv, atok)
                lntok = s_ac.inc(AC.activation(
                    lntmp[:, :], mv01[:, 1:2], AF.Ln, bias=epsv[:, 0:1], scale=1.0))
                W.wait(AC, s_ac, lntok)
                W.wait(AC, s_dv, ln_apply_tok[0])
                rstok = s_ac.inc(AC.activation(
                    rstd[:, :], lntmp[:, :], AF.Exp, bias=0.0, scale=-0.5))
                W.wait(DV, s_ac, rstok)
                W.wait(DV, s_dv, ngtok)
                W.wait(DV, s_out[sl], out_st[sl])
                W.wait(DV, s_xr[sl], s_xr[sl].n)
                ap_tok = s_dv.inc(DV.tensor_scalar(
                    xres[sl][:, :], y2[:, qc * D:(qc + 1) * D],
                    negm[:, 0:1], rstd[:, 0:1], OP.add, OP.mult))
                ln_apply_tok[0] = ap_tok
                xr_cons[sl] = ap_tok
                W.wait(SY, s_dv, ap_tok)
                SY.dma_start(out[trow + qc * 128: trow + (qc + 1) * 128, :],
                             xres[sl][:, :]).then_inc(s_out[sl].h, 16)
                s_out[sl].n += 16
                out_st[sl] = s_out[sl].n
        prev_dv_end[0] = s_dv.n

    W.wait(SY, s_out[0], s_out[0].n)
    W.wait(SY, s_out[1], s_out[1].n)

    if dbg:
        dbg_sem = Sem(st, nc, "s_dbg")
        W.wait(SY, s_dv, s_dv.n)
        W.wait(SY, s_ac, s_ac.n)
        for name in dbg:
            buf = dbg_bufs[name]
            shp = [128, buf.shape[1]]
            dt_ = nc.dram_tensor(f"dbg_{name}", shp, BF16, kind="ExternalOutput")
            dbg_sem.inc(SY.dma_start(dt_[:, :], buf[:, :]), 16)
        SY.wait_ge(dbg_sem.h, dbg_sem.n)

    st.close()
    return nc


def prep_inputs(L0_attn_x, L0_attn_w_q, L0_attn_w_kv, L0_attn_w_dense,
                L0_ffn_wq_gate, L0_ffn_s_gate, L0_ffn_wq_up, L0_ffn_s_up,
                L0_ffn_wq_down, L0_ffn_s_down):
    x = np.ascontiguousarray(np.asarray(L0_attn_x, dtype=np.float32))
    wq = np.ascontiguousarray(np.asarray(L0_attn_w_q, dtype=np.float32))
    wkv = np.ascontiguousarray(np.asarray(L0_attn_w_kv, dtype=np.float32))
    wd = np.ascontiguousarray(np.asarray(L0_attn_w_dense, dtype=np.float32))
    wg = np.asarray(L0_ffn_wq_gate, dtype=np.int32)
    wu = np.asarray(L0_ffn_wq_up, dtype=np.int32)
    wn = np.asarray(L0_ffn_wq_down, dtype=np.int32)
    sg = np.asarray(L0_ffn_s_gate, dtype=np.float32)
    su = np.asarray(L0_ffn_s_up, dtype=np.float32)
    sd = np.asarray(L0_ffn_s_down, dtype=np.float32)

    def perm_gate(w):
        return np.ascontiguousarray(
            w.reshape(HC, 128, 16, 128).transpose(0, 3, 2, 1)).reshape(HID, D)
    wg_p = perm_gate(wg)
    wu_p = perm_gate(wu)
    wn_p = np.ascontiguousarray(
        wn.reshape(D, HC, 128).transpose(1, 2, 0)).reshape(HID, D)
    sg_t = np.ascontiguousarray(sg.reshape(HC, 128).T)
    su_t = np.ascontiguousarray(su.reshape(HC, 128).T)
    sdb = np.ascontiguousarray(np.tile(sd.reshape(1, D), (128, 1)))

    mask = np.where(np.arange(128)[:, None] >= np.arange(128)[None, :],
                    np.float32(0.0), np.float32(NEGBIG)).astype(ml_dtypes.bfloat16)
    ident = np.eye(128, dtype=ml_dtypes.bfloat16)

    shared = {"wq_f": wq, "wkv_f": wkv, "wd_f": wd,
              "wg_i": wg_p, "wu_i": wu_p, "wn_i": wn_p,
              "sg_in": sg_t, "su_in": su_t, "sdb_in": sdb,
              "mask_in": mask, "id_in": ident}
    in_maps = []
    for c in range(NCORES):
        m = dict(shared)
        m["x_in"] = np.ascontiguousarray(
            x[c * SEQPC:(c + 1) * SEQPC].reshape(SEQPC * S, D))
        in_maps.append(m)
    return in_maps


def kernel(**inputs):
    in_maps = prep_inputs(**inputs)
    nc = build_program()
    res = run_bass_kernel_spmd(nc, in_maps, core_ids=list(range(NCORES)))
    outs = [r["out"].reshape(SEQPC, S, D) for r in res.results]
    return np.ascontiguousarray(np.concatenate(outs, axis=0).astype(np.float32))


# version 20
# speedup vs baseline: 1.0507x; 1.0507x over previous
import numpy as np
import ml_dtypes
import concourse.bass as bass
import concourse.mybir as mybir
from concourse.alu_op_type import AluOpType
from concourse.bass_utils import run_bass_kernel_spmd
from contextlib import ExitStack

F32 = mybir.dt.float32
BF16 = mybir.dt.bfloat16
I32 = mybir.dt.int32
AF = mybir.ActivationFunctionType
AX = mybir.AxisListType
OP = AluOpType

D = 2048
S = 2048
HEADS = 16
GROUPS = 4
KVC = 128
HID = 5632
HC = HID // 128
NCORES = 8
SEQPC = 2
TILE = 512
NT_SEQ = S // TILE
SCALE = KVC ** -0.5
EPS = 1e-5
NEGBIG = -1.0e30


class Sem:
    def __init__(self, stack, nc, name):
        self.h = stack.enter_context(nc.semaphore(name))
        self.n = 0

    def inc(self, instr, k=1):
        instr.then_inc(self.h, k)
        self.n += k
        return self.n


class Waiter:
    def __init__(self):
        self.last = {}

    def wait(self, eng, sem, val):
        if val <= 0:
            return
        key = (id(eng), id(sem))
        if self.last.get(key, 0) >= val:
            return
        eng.wait_ge(sem.h, val)
        self.last[key] = val


def build_program(n_tiles=SEQPC * NT_SEQ, dbg=None):
    nc = bass.Bass("TRN2", target_bir_lowering=False)
    NROW = SEQPC * S

    x_in = nc.dram_tensor("x_in", [NROW, D], F32, kind="ExternalInput")
    wq_f = nc.dram_tensor("wq_f", [D, D], F32, kind="ExternalInput")
    wkv_f = nc.dram_tensor("wkv_f", [D, 2 * GROUPS * KVC], F32, kind="ExternalInput")
    wd_f = nc.dram_tensor("wd_f", [D, D], F32, kind="ExternalInput")
    wg_i = nc.dram_tensor("wg_i", [HID, D], I32, kind="ExternalInput")
    wu_i = nc.dram_tensor("wu_i", [HID, D], I32, kind="ExternalInput")
    wn_i = nc.dram_tensor("wn_i", [HID, D], I32, kind="ExternalInput")
    sg_in = nc.dram_tensor("sg_in", [128, HC], F32, kind="ExternalInput")
    su_in = nc.dram_tensor("su_in", [128, HC], F32, kind="ExternalInput")
    sdb_in = nc.dram_tensor("sdb_in", [128, D], F32, kind="ExternalInput")
    mask_in = nc.dram_tensor("mask_in", [128, 128], BF16, kind="ExternalInput")
    id_in = nc.dram_tensor("id_in", [128, 128], BF16, kind="ExternalInput")
    out = nc.dram_tensor("out", [NROW, D], F32, kind="ExternalOutput")

    wq_b = nc.dram_tensor("wq_b", [D, D], BF16, kind="Internal")
    wkv_b = nc.dram_tensor("wkv_b", [D, 2 * GROUPS * KVC], BF16, kind="Internal")
    wd_b = nc.dram_tensor("wd_b", [D, D], BF16, kind="Internal")
    wgu_b = nc.dram_tensor("wgu_b", [HID, 2 * D], BF16, kind="Internal")
    wn_b = nc.dram_tensor("wn_b", [4, HC // 4, 128, 2048], BF16, kind="Internal")

    PE, DV, AC, SY, GP = nc.tensor, nc.vector, nc.scalar, nc.sync, nc.gpsimd
    W = Waiter()
    st = ExitStack()

    def mks(name):
        return Sem(st, nc, name)
    s_pe, s_dv, s_ac = mks("s_pe"), mks("s_dv"), mks("s_ac")
    s_cst = mks("s_cst")
    s_p0a = mks("s_p0a")
    s_cv = [mks(f"s_cv{i}") for i in range(2)]
    s_cs = [mks(f"s_cs{i}") for i in range(2)]
    s_pg = mks("s_pg")
    s_pu = mks("s_pu")
    s_pp = mks("s_pp")
    s_wq = [mks(f"s_wq{i}") for i in range(2)]
    s_wkv = [mks(f"s_wkv{i}") for i in range(2)]
    s_wd = [mks(f"s_wd{i}") for i in range(2)]
    s_wg = [mks(f"s_wg{i}") for i in range(3)]
    s_wn = [mks(f"s_wn{i}") for i in range(3)]
    s_xr = [mks(f"s_xr{i}") for i in range(2)]
    s_xb = [mks(f"s_xb{i}") for i in range(3)]
    s_xc = mks("s_xc")
    s_stw = mks("s_stw")
    s_out = [mks(f"s_out{i}") for i in range(2)]

    def sb(name, shape, dt):
        return st.enter_context(nc.sbuf_tensor(name, shape, dt))

    c_mask = sb("c_mask", [128, 128], BF16)
    c_id = sb("c_id", [128, 128], BF16)
    c_sg = sb("c_sg", [128, HC], F32)
    c_su = sb("c_su", [128, HC], F32)
    K_fm = sb("K_fm", [128, GROUPS * S], BF16)
    V_tm = sb("V_tm", [128, GROUPS * 16 * 128], BF16)
    Q_bf = sb("Q_bf", [128, HEADS * TILE], BF16)
    act_fm = sb("act_fm", [128, 16 * TILE], BF16)
    xbf = [sb(f"xbf{i}", [128, D], BF16) for i in range(3)]
    xres = [sb(f"xres{i}", [128, D], F32) for i in range(2)]
    x1b = sb("x1b", [128, 4 * D], BF16)
    gu = sb("gu", [128, HC * TILE], BF16)
    sdb_bf = sb("sdb_bf", [128, D], BF16)
    sums = sb("sums", [128, HEADS], F32)
    invs = sb("invs", [128, HEADS], F32)
    bnb = sb("bnb", [128, 24], F32)
    mv01 = sb("mv01", [128, 2], F32)
    negm = sb("negm", [128, 1], F32)
    lntmp = sb("lntmp", [128, 1], F32)
    rstd = sb("rstd", [128, 1], F32)
    epsv = sb("epsv", [128, 1], F32)
    shiftv = sb("shiftv", [128, 1], F32)
    wqb = [sb(f"wqb{i}", [128, 1024], BF16) for i in range(2)]
    wkvb = [sb(f"wkvb{i}", [128, 1024], BF16) for i in range(2)]
    wdb = [sb(f"wdb{i}", [128, 2048], BF16) for i in range(2)]
    wnb = [sb(f"wnb{i}", [128, D], BF16) for i in range(3)]
    m128 = sb("m128", [128, D // 2], BF16)

    s_cst.inc(SY.dma_start(c_mask[:, :], mask_in[:, :]), 16)
    s_cst.inc(SY.dma_start(c_id[:, :], id_in[:, :]), 16)
    s_cst.inc(SY.dma_start(c_sg[:, :], sg_in[:, :]), 16)
    s_cst.inc(SY.dma_start(c_su[:, :], su_in[:, :]), 16)
    CST_ALL = s_cst.n
    s_dv.inc(DV.memset(epsv[:, :], EPS))
    s_dv.inc(DV.memset(shiftv[:, :], -16.0))
    m128_tok = s_dv.inc(DV.memset(m128[:, :], -128.0))

    for i in range(4):
        s_p0a.inc(GP.dma_start(wq_b[i * 512:(i + 1) * 512, :],
                               wq_f[i * 512:(i + 1) * 512, :]), 16)
    P0_WQ = s_p0a.n
    for i in range(4):
        s_p0a.inc(GP.dma_start(wkv_b[i * 512:(i + 1) * 512, :],
                               wkv_f[i * 512:(i + 1) * 512, :]), 16)
    P0_WKV = s_p0a.n
    for i in range(4):
        s_p0a.inc(GP.dma_start(wd_b[i * 512:(i + 1) * 512, :],
                               wd_f[i * 512:(i + 1) * 512, :]), 16)
    P0_WD = s_p0a.n

    xbf_need = []

    def load_xbf(gchunk):
        trow0 = (gchunk // 4) * TILE + (gchunk % 4) * 128
        slot = gchunk % 3
        if gchunk >= 3:
            W.wait(GP, s_xc, gchunk - 2)
        GP.dma_start(xbf[slot][:, :], x_in[trow0: trow0 + 128, :]
                     ).then_inc(s_xb[slot].h, 16)
        s_xb[slot].n += 16
        xbf_need.append((slot, s_xb[slot].n))

    for _c in range(3):
        load_xbf(_c)

    s_cst.inc(GP.dma_start(sdb_bf[:, :], sdb_in[:, :]), 16)
    CST_SDB = s_cst.n
    n_chunks = n_tiles * 4
    for _c in range(3, min(6, n_chunks)):
        load_xbf(_c)
    wg_store_val = {}
    wu_store_val = {}
    wn_store_val = {}
    stw_pend_n = [0, 0, 0]
    next_xbf = [min(6, n_chunks)]

    wq_ld = [0]; wkv_ld = [0]; wd_ld = [0]; wn_ld = [0]; xr_ld = [0]
    wq_cons = [0, 0]; wkv_cons = [0, 0]; wd_cons = [0, 0]; wn_cons = [0, 0, 0]
    xr_cons = [0, 0]
    out_st = [0, 0]
    psum_id = [0]

    def psname(tag):
        psum_id[0] += 1
        return f"ps_{tag}_{psum_id[0]}"

    def stream_load(sem_list, cons, ld_cnt, buf_list, src_ap_fn, gate_extra=None):
        sl = ld_cnt[0] % 2
        W.wait(SY, s_pe, cons[sl])
        if gate_extra is not None:
            gate_extra(sl)
        SY.dma_start(buf_list[sl][:, :], src_ap_fn()).then_inc(sem_list[sl].h, 16)
        sem_list[sl].n += 16
        ld_cnt[0] += 1
        return sl, sem_list[sl].n

    dbg = dbg or []
    dbg_bufs = {"K_fm": K_fm, "V_tm": V_tm, "Q_bf": Q_bf, "act_fm": act_fm,
                "x1b": x1b, "gu": gu}
    ln2_stt_last = [0]
    last_ctx_pe = [0]
    ln_apply_tok = [0]
    prev_dv_end = [0]

    for t in range(n_tiles):
        tt = t % NT_SEQ
        trow = t * TILE

        while next_xbf[0] < min(4 * t + 6, n_chunks):
            load_xbf(next_xbf[0])
            next_xbf[0] += 1
        with ExitStack() as ps:
            ps.enter_context(nc.named_scope(f"xT{t}", notify=True))
            pxt = [ps.enter_context(nc.psum_tensor(psname("xt"), [128, 1024], BF16))
                   for _ in range(2)]
            W.wait(PE, s_dv, prev_dv_end[0])
            W.wait(PE, s_stw, s_stw.n)
            evac_tok = {}
            for c in range(4):
                slot, semval = xbf_need[t * 4 + c]
                W.wait(PE, s_xb[slot], semval)
                if t == 0 and c == 0:
                    W.wait(PE, s_cst, CST_ALL)
                for g4 in range(4):
                    bank = (c * 4 + g4) % 2
                    if (c * 4 + g4) >= 2:
                        W.wait(PE, s_dv, evac_tok.get(c * 4 + g4 - 2, 0))
                    for j in range(4):
                        hc = g4 * 4 + j
                        ins = PE.transpose(pxt[bank][:, j * 128:(j + 1) * 128],
                                           xbf[slot][:, hc * 128:(hc + 1) * 128],
                                           c_id[:, :])
                    pe_tok = s_pe.inc(ins)
                    W.wait(DV, s_pe, pe_tok)
                    dst = act_fm[:, :].rearrange("p (hc t) -> p hc t", hc=16)[
                        :, g4 * 4:(g4 + 1) * 4, c * 128:(c + 1) * 128]
                    evac_tok[c * 4 + g4] = s_dv.inc(DV.tensor_copy(dst, pxt[bank][:, 0:512]))
                nc.tensor.nop().then_inc(s_xc.h, 1)
                s_xc.n += 1
            xfm_tok = s_dv.n

        with ExitStack() as ps:
            ps.enter_context(nc.named_scope(f"qkv{t}", notify=True))
            pq = [ps.enter_context(nc.psum_tensor(psname("q"), [128, 512], F32))
                  for _ in range(8)]
            W.wait(PE, s_dv, xfm_tok)
            qev_tok = 0
            for half in range(2):
                if half == 1:
                    W.wait(PE, s_ac, qev_tok)
                for hc in range(16):
                    def gate(sl):
                        if t == 0 and wq_ld[0] < 2:
                            W.wait(SY, s_p0a, P0_WQ)
                    sl, need = stream_load(
                        s_wq, wq_cons, wq_ld, wqb,
                        lambda: wq_b[hc * 128:(hc + 1) * 128,
                                     half * 1024:(half + 1) * 1024], gate)
                    W.wait(PE, s_wq[sl], need)
                    for ec in range(8):
                        ins = PE.matmul(pq[ec][:, :],
                                        wqb[sl][:, ec * 128:(ec + 1) * 128],
                                        act_fm[:, hc * 512:(hc + 1) * 512],
                                        start=(hc == 0), stop=(hc == 15))
                    wq_cons[sl] = s_pe.inc(ins)
                mm_tok = s_pe.n
                W.wait(AC, s_pe, mm_tok)
                for ec in range(8):
                    h_abs = half * 8 + ec
                    ins = AC.activation(
                        Q_bf[:, :].rearrange("p (h t) -> p h t", h=16)[:, h_abs, :],
                        pq[ec][:, :], AF.Copy, bias=0.0, scale=SCALE)
                qev_tok = s_ac.inc(ins)
            W.wait(PE, s_ac, qev_tok)
            for hc in range(16):
                def gate(sl):
                    if t == 0 and wkv_ld[0] < 2:
                        W.wait(SY, s_p0a, P0_WKV)
                sl, need = stream_load(
                    s_wkv, wkv_cons, wkv_ld, wkvb,
                    lambda: wkv_b[hc * 128:(hc + 1) * 128, :], gate)
                W.wait(PE, s_wkv[sl], need)
                for dc in range(4):
                    ins = PE.matmul(pq[dc][:, :], wkvb[sl][:, dc * 128:(dc + 1) * 128],
                                    act_fm[:, hc * 512:(hc + 1) * 512],
                                    start=(hc == 0), stop=(hc == 15))
                for tc in range(4):
                    ins = PE.matmul(pq[4 + tc][:, :],
                                    act_fm[:, hc * 512 + tc * 128: hc * 512 + (tc + 1) * 128],
                                    wkvb[sl][:, 512:1024],
                                    start=(hc == 0), stop=(hc == 15))
                wkv_cons[sl] = s_pe.inc(ins)
            kv_tok = s_pe.n
            W.wait(DV, s_pe, kv_tok)
            for dc in range(4):
                ins = DV.tensor_copy(K_fm[:, dc * S + tt * 512: dc * S + (tt + 1) * 512],
                                     pq[dc][:, :])
            for tc in range(4):
                kc = tt * 4 + tc
                dst = V_tm[:, :].rearrange("p (g kc d) -> p g kc d", g=4, kc=16)[
                    :, :, kc, :]
                ins = DV.tensor_copy(dst, pq[4 + tc][:, :].rearrange(
                    "p (g d) -> p g d", g=4))
            kvev_tok = s_dv.inc(ins)

        stt_prev = [0]
        for qp in range(2):
            with ExitStack() as pairsc:
                pairsc.enter_context(nc.named_scope(f"attn{t}_{qp}", notify=True))
                ctx2 = [pairsc.enter_context(
                    nc.sbuf_tensor(psname("ctx"), [128, 2048], BF16)) for _ in range(2)]
                ctx_done = 0
                exp_all = 0
                for qq in range(2):
                    qc = qp * 2 + qq
                    qcg = tt * 4 + qc
                    klen = (qcg + 1) * 128
                    nkc = qcg + 1
                    ctx = ctx2[qq]
                    with ExitStack() as qs:
                        probs = [qs.enter_context(nc.sbuf_tensor(
                            psname("probs"), [128, 2048], BF16)) for _ in range(2)]
                        probsT = qs.enter_context(nc.sbuf_tensor(
                            psname("probsT"), [128, 2048], BF16))
                        ps_s = qs.enter_context(nc.psum_tensor(psname("s"), [128, 2048], F32))
                        ps_pt = [qs.enter_context(nc.psum_tensor(
                            psname("pt"), [128, 1024], BF16)) for _ in range(2)]
                        ps_cx = [qs.enter_context(nc.psum_tensor(
                            psname("cx"), [128, 512], F32)) for _ in range(2)]

                        W.wait(PE, s_dv, kvev_tok)
                        W.wait(PE, s_dv, stt_prev[0])
                        W.wait(PE, s_ac, s_ac.n)
                        exp_tok = {}; ptev_tok = {}; ctx_tok = {}; ctxev_tok = {}
                        norm_tok = {}
                        for it in range(HEADS + 1):
                            if it < HEADS:
                                h = it
                                g = h // 4
                                if h > 0:
                                    W.wait(PE, s_ac, exp_tok[h - 1])
                                lhsQ = Q_bf[:, h * 512 + qc * 128: h * 512 + (qc + 1) * 128]
                                base = klen - 128
                                blo = (base // 512) * 512
                                if base % 512 != 0:
                                    PE.matmul(ps_s[:, blo:base], lhsQ,
                                              K_fm[:, g * S + blo: g * S + base],
                                              start=True, stop=True)
                                ins = PE.matmul(ps_s[:, base:klen], lhsQ,
                                                K_fm[:, g * S + base: g * S + klen],
                                                start=True, stop=True)
                                scd_tok = s_pe.inc(ins)
                                for j in range(blo // 512):
                                    ins = PE.matmul(
                                        ps_s[:, j * 512:(j + 1) * 512], lhsQ,
                                        K_fm[:, g * S + j * 512: g * S + (j + 1) * 512],
                                        start=True, stop=True)
                                sc_tok = s_pe.inc(ins) if blo > 0 else scd_tok
                                W.wait(DV, s_pe, scd_tok)
                                mtok = s_dv.inc(DV.tensor_tensor(
                                    ps_s[:, klen - 128:klen], ps_s[:, klen - 128:klen],
                                    c_mask[:, :], OP.add))
                                sc_mm = (sc_tok, mtok)
                            if it >= 1:
                                h = it - 1
                                g = h // 4
                                W.wait(PE, s_dv, norm_tok[h])
                                nbat = (nkc + 3) // 4
                                for b in range(nbat):
                                    if b >= 2:
                                        W.wait(PE, s_dv, ptev_tok[(h, b - 2)])
                                    kcs = list(range(b * 4, min(nkc, b * 4 + 4)))
                                    for kk in kcs:
                                        ins = PE.transpose(
                                            ps_pt[b % 2][:, (kk % 4) * 128:(kk % 4 + 1) * 128],
                                            probs[h % 2][:, kk * 128:(kk + 1) * 128],
                                            c_id[:, :])
                                    ptok = s_pe.inc(ins)
                                    W.wait(DV, s_pe, ptok)
                                    if b == 0:
                                        W.wait(DV, s_pe, ctx_tok[h - 1] if h > 0
                                               else last_ctx_pe[0])
                                    wdt = 128 * len(kcs)
                                    ptev_tok[(h, b)] = s_dv.inc(DV.tensor_copy(
                                        probsT[:, b * 512: b * 512 + wdt],
                                        ps_pt[b % 2][:, 0:wdt]))
                                if h % 4 == 0:
                                    W.wait(PE, s_dv, ctxev_tok.get(g - 2, 0))
                                for kk in range(nkc):
                                    W.wait(PE, s_dv, ptev_tok[(h, kk // 4)])
                                    ins = PE.matmul(
                                        ps_cx[g % 2][:, (h % 4) * 128:(h % 4 + 1) * 128],
                                        V_tm[:, (g * 16 + kk) * 128:(g * 16 + kk + 1) * 128],
                                        probsT[:, kk * 128:(kk + 1) * 128],
                                        start=(kk == 0), stop=(kk == nkc - 1))
                                ctx_tok[h] = s_pe.inc(ins)
                                last_ctx_pe[0] = ctx_tok[h]
                                if h % 4 == 3:
                                    W.wait(DV, s_pe, ctx_tok[h])
                                    ctxev_tok[g] = s_dv.inc(DV.tensor_copy(
                                        ctx[:, g * 512:(g + 1) * 512], ps_cx[g % 2][:, :]))
                            if it < HEADS:
                                h = it
                                sc_tok, mtok = sc_mm
                                W.wait(AC, s_dv, mtok)
                                W.wait(AC, s_pe, sc_tok)
                                exp_tok[h] = s_ac.inc(AC.activation(
                                    probs[h % 2][:, 0:klen], ps_s[:, 0:klen], AF.Exp,
                                    bias=shiftv[:, 0:1], scale=1.0,
                                    accum_out=sums[:, h:h + 1]))
                                W.wait(DV, s_ac, exp_tok[h])
                                rc_tok = s_dv.inc(DV.reciprocal(invs[:, h:h + 1],
                                                                sums[:, h:h + 1]))
                                W.wait(DV, s_dv, rc_tok)
                                norm_tok[h] = s_dv.inc(DV.tensor_scalar(
                                    probs[h % 2][:, 0:klen], probs[h % 2][:, 0:klen],
                                    invs[:, h:h + 1], None, OP.mult))
                        ctx_done = s_dv.n
                        exp_all = s_ac.n
                        if dbg and "taps" in dbg and t == 0:
                            dt_ = nc.dram_tensor(f"dbg_ctx_{qc}", [128, 2048], BF16,
                                                 kind="ExternalOutput")
                            dt2_ = nc.dram_tensor(f"dbg_probsT_{qc}", [128, 2048], BF16,
                                                  kind="ExternalOutput")
                            dt3_ = nc.dram_tensor(f"dbg_sums_{qc}", [128, HEADS], F32,
                                                  kind="ExternalOutput")
                            W.wait(SY, s_dv, ctx_done)
                            SY.dma_start(dt_[:, :], ctx[:, :]).then_inc(s_cst.h, 16)
                            SY.dma_start(dt2_[:, :], probsT[:, :]).then_inc(s_cst.h, 16)
                            SY.dma_start(dt3_[:, :], sums[:, :]).then_inc(s_cst.h, 16)
                            s_cst.n += 48
                            W.wait(SY, s_cst, s_cst.n)

                with ExitStack() as ps2:
                    ps2.enter_context(nc.named_scope(f"dense{t}_{qp}", notify=True))
                    ps_do = [ps2.enter_context(nc.psum_tensor(psname("do"), [128, 2048], F32))
                             for _ in range(2)]
                    xr_need = [0, 0]
                    for qq in range(2):
                        qc = qp * 2 + qq
                        sl = qc % 2
                        W.wait(SY, s_dv, xr_cons[sl])
                        W.wait(SY, s_out[sl], out_st[sl])
                        SY.dma_start(xres[sl][:, :],
                                     x_in[trow + qc * 128: trow + (qc + 1) * 128, :]
                                     ).then_inc(s_xr[sl].h, 16)
                        s_xr[sl].n += 16
                        xr_need[qq] = s_xr[sl].n
                        xr_ld[0] += 1
                    W.wait(PE, s_dv, ctx_done)
                    W.wait(PE, s_ac, exp_all)
                    for dc in range(16):
                        def gate(wsl):
                            if t == 0 and wd_ld[0] < 2:
                                W.wait(SY, s_p0a, P0_WD)
                        wsl, need = stream_load(
                            s_wd, wd_cons, wd_ld, wdb,
                            lambda: wd_b[dc * 128:(dc + 1) * 128, :], gate)
                        W.wait(PE, s_wd[wsl], need)
                        for qq in range(2):
                            for n in range(4):
                                ins = PE.matmul(
                                    ps_do[qq][:, n * 512:(n + 1) * 512],
                                    ctx2[qq][:, dc * 128:(dc + 1) * 128],
                                    wdb[wsl][:, n * 512:(n + 1) * 512],
                                    start=(dc == 0), stop=(dc == 15))
                        wd_cons[wsl] = s_pe.inc(ins)
                    do_tok = s_pe.n
                    for qq in range(2):
                        qc = qp * 2 + qq
                        sl = qc % 2
                        W.wait(DV, s_pe, do_tok)
                        W.wait(DV, s_xr[sl], xr_need[qq])
                        stt_tok = s_dv.inc(DV.scalar_tensor_tensor(
                            xres[sl][:, :], ps_do[qq][:, :], 1.0, xres[sl][:, :],
                            OP.mult, OP.add))
                        xr_cons[sl] = stt_tok
                        stt_prev[0] = stt_tok
                        if dbg and "taps" in dbg and t == 0:
                            dt_ = nc.dram_tensor(f"dbg_ln1pre_{qc}", [128, D], F32,
                                                 kind="ExternalOutput")
                            W.wait(SY, s_dv, stt_tok)
                            SY.dma_start(dt_[:, :], xres[(qp*2+qq) % 2][:, :]).then_inc(s_cst.h, 16)
                            s_cst.n += 16
                            W.wait(DV, s_cst, s_cst.n)
                        for j in range(4):
                            ins = DV.bn_stats(bnb[:, j * 6:(j + 1) * 6],
                                              xres[sl][:, j * 512:(j + 1) * 512])
                        btok = s_dv.inc(ins)
                        W.wait(DV, s_dv, btok)
                        atok = s_dv.inc(DV.bn_aggr(
                            mv01[:, :], bnb[:, :].rearrange("p (n t) -> p n t", t=3)))
                        W.wait(DV, s_dv, atok)
                        ngtok = s_dv.inc(DV.tensor_scalar(
                            negm[:, :], mv01[:, 0:1], -1.0, None, OP.mult))
                        W.wait(AC, s_dv, atok)
                        lntok = s_ac.inc(AC.activation(
                            lntmp[:, :], mv01[:, 1:2], AF.Ln, bias=epsv[:, 0:1], scale=1.0))
                        W.wait(AC, s_ac, lntok)
                        W.wait(AC, s_dv, ln_apply_tok[0])
                        rstok = s_ac.inc(AC.activation(
                            rstd[:, :], lntmp[:, :], AF.Exp, bias=0.0, scale=-0.5))
                        W.wait(DV, s_ac, rstok)
                        W.wait(DV, s_dv, ngtok)
                        ln_apply_tok[0] = s_dv.inc(DV.tensor_scalar(
                            x1b[:, qc * D:(qc + 1) * D], xres[sl][:, :],
                            negm[:, 0:1], rstd[:, 0:1], OP.add, OP.mult))
                        xr_cons[sl] = ln_apply_tok[0]
        ln1_done = s_dv.n

        with ExitStack() as ps:
            ps.enter_context(nc.named_scope(f"x1T{t}", notify=True))
            pxt = [ps.enter_context(nc.psum_tensor(psname("x1t"), [128, 1024], BF16))
                   for _ in range(2)]
            W.wait(PE, s_dv, ln1_done)
            evac_tok = {}
            for qc in range(4):
                for g4 in range(4):
                    bank = (qc * 4 + g4) % 2
                    if (qc * 4 + g4) >= 2:
                        W.wait(PE, s_dv, evac_tok.get(qc * 4 + g4 - 2, 0))
                    for j in range(4):
                        dc = g4 * 4 + j
                        ins = PE.transpose(pxt[bank][:, j * 128:(j + 1) * 128],
                                           x1b[:, qc * D + dc * 128: qc * D + (dc + 1) * 128],
                                           c_id[:, :])
                    pe_tok = s_pe.inc(ins)
                    W.wait(DV, s_pe, pe_tok)
                    dst = act_fm[:, :].rearrange("p (hc t) -> p hc t", hc=16)[
                        :, g4 * 4:(g4 + 1) * 4, qc * 128:(qc + 1) * 128]
                    evac_tok[qc * 4 + g4] = s_dv.inc(DV.tensor_copy(dst, pxt[bank][:, 0:512]))
            x1fm_tok = s_dv.n

        with ExitStack() as ps:
            ps.enter_context(nc.named_scope(f"gate{t}", notify=True))
            pg = [ps.enter_context(nc.psum_tensor(psname("g"), [128, 512], F32))
                  for _ in range(2)]
            pu = [ps.enter_context(nc.psum_tensor(psname("u"), [128, 512], F32))
                  for _ in range(2)]
            wgub = [ps.enter_context(nc.sbuf_tensor(psname("wgub"), [128, 2 * D], BF16))
                    for _ in range(2)]
            gsil = [ps.enter_context(nc.sbuf_tensor(psname("gsil"), [128, TILE], BF16))
                    for _ in range(2)]
            wgu_cons = [s_pe.n] * 2
            stw_pend = [0] * 2
            W.wait(SY, s_dv, s_dv.n)
            W.wait(PE, s_dv, x1fm_tok)
            sil_tok = {}; gu_tok = {}
            for hc in range(HC):
                sl = hc % 2
                if t == 0:
                    W.wait(GP, s_pe, wgu_cons[sl])
                    W.wait(GP, s_stw, stw_pend[sl])
                    W.wait(GP, s_dv, m128_tok)
                    for _q in range(4):
                        GP.dma_start(wgub[sl][:, _q * 1024:(_q + 1) * 1024],
                                     m128[:, :]).then_inc(s_wg[sl].h, 16)
                    s_wg[sl].n += 64
                    W.wait(GP, s_wg[sl], s_wg[sl].n)
                    GP.dma_start(wgub[sl][:, 0:D], wg_i[hc * 128:(hc + 1) * 128, :],
                                 accum_op=OP.add).then_inc(s_wg[sl].h, 16)
                    GP.dma_start(wgub[sl][:, D:2 * D], wu_i[hc * 128:(hc + 1) * 128, :],
                                 accum_op=OP.add).then_inc(s_wg[sl].h, 16)
                    s_wg[sl].n += 32
                    need = s_wg[sl].n
                    W.wait(GP, s_wg[sl], need)
                    GP.dma_start(wgu_b[hc * 128:(hc + 1) * 128, :], wgub[sl][:, :]
                                 ).then_inc(s_stw.h, 16)
                    s_stw.n += 16
                    wg_store_val[hc] = s_stw.n
                    stw_pend[sl] = s_stw.n
                else:
                    W.wait(SY, s_pe, wgu_cons[sl])
                    if t == 1:
                        W.wait(SY, s_stw, wg_store_val[hc])
                    SY.dma_start(wgub[sl][:, :], wgu_b[hc * 128:(hc + 1) * 128, :]
                                 ).then_inc(s_wg[sl].h, 16)
                    s_wg[sl].n += 16
                    need = s_wg[sl].n
                W.wait(PE, s_wg[sl], need)
                if hc >= 2:
                    W.wait(PE, s_ac, sil_tok[hc - 2])
                    W.wait(PE, s_dv, gu_tok[hc - 2])
                psl = hc % 2
                for dc in range(16):
                    ins = PE.matmul(pg[psl][:, :], wgub[sl][:, dc * 128:(dc + 1) * 128],
                                    act_fm[:, dc * 512:(dc + 1) * 512],
                                    start=(dc == 0), stop=(dc == 15))
                g_tok = s_pe.inc(ins)
                for dc in range(16):
                    ins = PE.matmul(pu[psl][:, :],
                                    wgub[sl][:, D + dc * 128: D + (dc + 1) * 128],
                                    act_fm[:, dc * 512:(dc + 1) * 512],
                                    start=(dc == 0), stop=(dc == 15))
                u_tok = s_pe.inc(ins)
                wgu_cons[sl] = u_tok
                W.wait(AC, s_pe, g_tok)
                sil_tok[hc] = s_ac.inc(AC.activation(
                    gsil[psl][:, :], pg[psl][:, :], AF.Silu,
                    bias=0.0, scale=c_sg[:, hc:hc + 1]))
                W.wait(DV, s_pe, u_tok)
                W.wait(DV, s_ac, sil_tok[hc])
                gu_tok[hc] = s_dv.inc(DV.scalar_tensor_tensor(
                    gu[:, hc * 512:(hc + 1) * 512], pu[psl][:, :],
                    c_su[:, hc:hc + 1], gsil[psl][:, :], OP.mult, OP.mult))
            gu_done = s_dv.n

        with ExitStack() as ps:
            ps.enter_context(nc.named_scope(f"down{t}", notify=True))
            pdn = [ps.enter_context(nc.psum_tensor(psname("dn"), [128, 512], F32))
                   for _ in range(4)]
            y2 = ps.enter_context(nc.sbuf_tensor(psname("y2"), [128, 4 * D], BF16))
            bnb4 = ps.enter_context(nc.sbuf_tensor(psname("bnb4"), [128, 96], F32))
            W.wait(PE, s_dv, gu_done)
            if t == 0:
                W.wait(DV, s_stw, s_stw.n)
            ev_tok = {}
            for eq in range(4):
                for hcg in range(11):
                    if t == 0:
                        wsl = wn_ld[0] % 3
                        W.wait(GP, s_pe, wn_cons[wsl])
                        W.wait(GP, s_stw, stw_pend_n[wsl])
                        W.wait(GP, s_dv, m128_tok)
                        for _q in range(2):
                            GP.dma_start(wnb[wsl][:, _q * 1024:(_q + 1) * 1024],
                                         m128[:, :]).then_inc(s_wn[wsl].h, 16)
                        s_wn[wsl].n += 32
                        W.wait(GP, s_wn[wsl], s_wn[wsl].n)
                        GP.dma_start(
                            wnb[wsl][:, :],
                            wn_i[:, :].rearrange("(hc p) e -> p hc e", p=128)[
                                :, hcg * 4:(hcg + 1) * 4,
                                eq * 512:(eq + 1) * 512],
                            accum_op=OP.add).then_inc(s_wn[wsl].h, 16)
                        s_wn[wsl].n += 16
                        need = s_wn[wsl].n
                        W.wait(GP, s_wn[wsl], need)
                        GP.dma_start(wn_b[eq, hcg, :, :], wnb[wsl][:, :]
                                     ).then_inc(s_stw.h, 16)
                        s_stw.n += 16
                        wn_store_val[(eq, hcg)] = s_stw.n
                        stw_pend_n[wsl] = s_stw.n
                        wn_ld[0] += 1
                    else:
                        wsl = wn_ld[0] % 3
                        W.wait(SY, s_pe, wn_cons[wsl])
                        if t == 1:
                            W.wait(SY, s_stw, s_stw.n)
                        SY.dma_start(wnb[wsl][:, :], wn_b[eq, hcg, :, :]
                                     ).then_inc(s_wn[wsl].h, 16)
                        s_wn[wsl].n += 16
                        need = s_wn[wsl].n
                        wn_ld[0] += 1
                    W.wait(PE, s_wn[wsl], need)
                    if hcg == 0 and eq > 0:
                        W.wait(PE, s_dv, ev_tok[eq - 1])
                    for j in range(4):
                        hc = hcg * 4 + j
                        for qc in range(4):
                            ins = PE.matmul(
                                pdn[qc][:, :],
                                gu[:, hc * 512 + qc * 128: hc * 512 + (qc + 1) * 128],
                                wnb[wsl][:, j * 512:(j + 1) * 512],
                                start=(hc == 0), stop=(hc == HC - 1))
                    wn_cons[wsl] = s_pe.inc(ins)
                dn_tok = s_pe.n
                W.wait(DV, s_pe, dn_tok)
                W.wait(DV, s_cst, CST_SDB)
                for qc in range(4):
                    sli = slice(qc * D + eq * 512, qc * D + (eq + 1) * 512)
                    s_dv.inc(DV.tensor_tensor(
                        y2[:, sli], pdn[qc][:, :],
                        sdb_bf[:, eq * 512:(eq + 1) * 512], OP.mult))
                    s_dv.inc(DV.tensor_tensor(y2[:, sli], y2[:, sli],
                                              x1b[:, sli], OP.add))
                    ins = DV.bn_stats(bnb4[:, qc * 24 + eq * 6: qc * 24 + (eq + 1) * 6],
                                      y2[:, sli])
                ev_tok[eq] = s_dv.inc(ins)
            for qc in range(4):
                sl = qc % 2
                W.wait(DV, s_dv, ev_tok[3])
                atok = s_dv.inc(DV.bn_aggr(
                    mv01[:, :],
                    bnb4[:, qc * 24:(qc + 1) * 24].rearrange("p (n t) -> p n t", t=3)))
                W.wait(DV, s_dv, atok)
                ngtok = s_dv.inc(DV.tensor_scalar(
                    negm[:, :], mv01[:, 0:1], -1.0, None, OP.mult))
                W.wait(AC, s_dv, atok)
                lntok = s_ac.inc(AC.activation(
                    lntmp[:, :], mv01[:, 1:2], AF.Ln, bias=epsv[:, 0:1], scale=1.0))
                W.wait(AC, s_ac, lntok)
                W.wait(AC, s_dv, ln_apply_tok[0])
                rstok = s_ac.inc(AC.activation(
                    rstd[:, :], lntmp[:, :], AF.Exp, bias=0.0, scale=-0.5))
                W.wait(DV, s_ac, rstok)
                W.wait(DV, s_dv, ngtok)
                W.wait(DV, s_out[sl], out_st[sl])
                W.wait(DV, s_xr[sl], s_xr[sl].n)
                ap_tok = s_dv.inc(DV.tensor_scalar(
                    xres[sl][:, :], y2[:, qc * D:(qc + 1) * D],
                    negm[:, 0:1], rstd[:, 0:1], OP.add, OP.mult))
                ln_apply_tok[0] = ap_tok
                xr_cons[sl] = ap_tok
                W.wait(SY, s_dv, ap_tok)
                SY.dma_start(out[trow + qc * 128: trow + (qc + 1) * 128, :],
                             xres[sl][:, :]).then_inc(s_out[sl].h, 16)
                s_out[sl].n += 16
                out_st[sl] = s_out[sl].n
        prev_dv_end[0] = s_dv.n

    W.wait(SY, s_out[0], s_out[0].n)
    W.wait(SY, s_out[1], s_out[1].n)

    if dbg:
        dbg_sem = Sem(st, nc, "s_dbg")
        W.wait(SY, s_dv, s_dv.n)
        W.wait(SY, s_ac, s_ac.n)
        for name in dbg:
            buf = dbg_bufs[name]
            shp = [128, buf.shape[1]]
            dt_ = nc.dram_tensor(f"dbg_{name}", shp, BF16, kind="ExternalOutput")
            dbg_sem.inc(SY.dma_start(dt_[:, :], buf[:, :]), 16)
        SY.wait_ge(dbg_sem.h, dbg_sem.n)

    st.close()
    return nc


def prep_inputs(L0_attn_x, L0_attn_w_q, L0_attn_w_kv, L0_attn_w_dense,
                L0_ffn_wq_gate, L0_ffn_s_gate, L0_ffn_wq_up, L0_ffn_s_up,
                L0_ffn_wq_down, L0_ffn_s_down):
    x = np.ascontiguousarray(np.asarray(L0_attn_x, dtype=np.float32))
    wq = np.ascontiguousarray(np.asarray(L0_attn_w_q, dtype=np.float32))
    wkv = np.ascontiguousarray(np.asarray(L0_attn_w_kv, dtype=np.float32))
    wd = np.ascontiguousarray(np.asarray(L0_attn_w_dense, dtype=np.float32))
    wg = np.asarray(L0_ffn_wq_gate, dtype=np.int32)
    wu = np.asarray(L0_ffn_wq_up, dtype=np.int32)
    wn = np.asarray(L0_ffn_wq_down, dtype=np.int32)
    sg = np.asarray(L0_ffn_s_gate, dtype=np.float32)
    su = np.asarray(L0_ffn_s_up, dtype=np.float32)
    sd = np.asarray(L0_ffn_s_down, dtype=np.float32)

    def perm_gate(w):
        return np.ascontiguousarray(
            w.reshape(HC, 128, 16, 128).transpose(0, 3, 2, 1)).reshape(HID, D)
    wg_p = perm_gate(wg)
    wu_p = perm_gate(wu)
    wn_p = np.ascontiguousarray(
        wn.reshape(D, HC, 128).transpose(1, 2, 0)).reshape(HID, D)
    sg_t = np.ascontiguousarray(sg.reshape(HC, 128).T)
    su_t = np.ascontiguousarray(su.reshape(HC, 128).T)
    sdb = np.ascontiguousarray(np.tile(sd.reshape(1, D), (128, 1)))

    mask = np.where(np.arange(128)[:, None] >= np.arange(128)[None, :],
                    np.float32(0.0), np.float32(NEGBIG)).astype(ml_dtypes.bfloat16)
    ident = np.eye(128, dtype=ml_dtypes.bfloat16)

    shared = {"wq_f": wq, "wkv_f": wkv, "wd_f": wd,
              "wg_i": wg_p, "wu_i": wu_p, "wn_i": wn_p,
              "sg_in": sg_t, "su_in": su_t, "sdb_in": sdb,
              "mask_in": mask, "id_in": ident}
    in_maps = []
    for c in range(NCORES):
        m = dict(shared)
        m["x_in"] = np.ascontiguousarray(
            x[c * SEQPC:(c + 1) * SEQPC].reshape(SEQPC * S, D))
        in_maps.append(m)
    return in_maps


def kernel(**inputs):
    in_maps = prep_inputs(**inputs)
    nc = build_program()
    res = run_bass_kernel_spmd(nc, in_maps, core_ids=list(range(NCORES)))
    outs = [r["out"].reshape(SEQPC, S, D) for r in res.results]
    return np.ascontiguousarray(np.concatenate(outs, axis=0).astype(np.float32))


# version 25 (final)
# speedup vs baseline: 1.1783x; 1.1215x over previous
import numpy as np
import ml_dtypes
import concourse.bass as bass
import concourse.mybir as mybir
from concourse.alu_op_type import AluOpType
from concourse.bass_utils import run_bass_kernel_spmd
from contextlib import ExitStack

F32 = mybir.dt.float32
BF16 = mybir.dt.bfloat16
I32 = mybir.dt.int32
AF = mybir.ActivationFunctionType
AX = mybir.AxisListType
OP = AluOpType

D = 2048
S = 2048
HEADS = 16
GROUPS = 4
KVC = 128
HID = 5632
HC = HID // 128
NCORES = 8
SEQPC = 2
TILE = 512
NT_SEQ = S // TILE
SCALE = KVC ** -0.5
EPS = 1e-5
NEGBIG = -1.0e30


class Sem:
    def __init__(self, stack, nc, name):
        self.h = stack.enter_context(nc.semaphore(name))
        self.n = 0

    def inc(self, instr, k=1):
        instr.then_inc(self.h, k)
        self.n += k
        return self.n


class Waiter:
    def __init__(self):
        self.last = {}

    def wait(self, eng, sem, val):
        if val <= 0:
            return
        key = (id(eng), id(sem))
        if self.last.get(key, 0) >= val:
            return
        eng.wait_ge(sem.h, val)
        self.last[key] = val


def build_program(n_tiles=SEQPC * NT_SEQ, dbg=None):
    nc = bass.Bass("TRN2", target_bir_lowering=False)
    NROW = SEQPC * S

    x_in = nc.dram_tensor("x_in", [NROW, D], F32, kind="ExternalInput")
    wq_f = nc.dram_tensor("wq_f", [D, D], F32, kind="ExternalInput")
    wkv_f = nc.dram_tensor("wkv_f", [D, 2 * GROUPS * KVC], F32, kind="ExternalInput")
    wd_f = nc.dram_tensor("wd_f", [D, D], F32, kind="ExternalInput")
    wg_i = nc.dram_tensor("wg_i", [HID, D], I32, kind="ExternalInput")
    wu_i = nc.dram_tensor("wu_i", [HID, D], I32, kind="ExternalInput")
    wn_i = nc.dram_tensor("wn_i", [HID, D], I32, kind="ExternalInput")
    sg_in = nc.dram_tensor("sg_in", [128, HC], F32, kind="ExternalInput")
    su_in = nc.dram_tensor("su_in", [128, HC], F32, kind="ExternalInput")
    sdb_in = nc.dram_tensor("sdb_in", [128, D], F32, kind="ExternalInput")
    mask_in = nc.dram_tensor("mask_in", [128, 128], BF16, kind="ExternalInput")
    id_in = nc.dram_tensor("id_in", [128, 128], BF16, kind="ExternalInput")
    out = nc.dram_tensor("out", [NROW, D], F32, kind="ExternalOutput")

    wq_b = nc.dram_tensor("wq_b", [D, D], BF16, kind="Internal")
    wkv_b = nc.dram_tensor("wkv_b", [D, 2 * GROUPS * KVC], BF16, kind="Internal")
    wd_b = nc.dram_tensor("wd_b", [D, D], BF16, kind="Internal")
    wgu_b = nc.dram_tensor("wgu_b", [HID, 2 * D], BF16, kind="Internal")
    wn_b = nc.dram_tensor("wn_b", [4, HC // 4, 128, 2048], BF16, kind="Internal")

    PE, DV, AC, SY, GP = nc.tensor, nc.vector, nc.scalar, nc.sync, nc.gpsimd
    W = Waiter()
    st = ExitStack()

    def mks(name):
        return Sem(st, nc, name)
    s_pe, s_dv, s_ac = mks("s_pe"), mks("s_dv"), mks("s_ac")
    s_cst = mks("s_cst")
    s_p0a = mks("s_p0a")
    s_cv = [mks(f"s_cv{i}") for i in range(2)]
    s_cs = [mks(f"s_cs{i}") for i in range(2)]
    s_wq = [mks(f"s_wq{i}") for i in range(2)]
    s_wkv = [mks(f"s_wkv{i}") for i in range(2)]
    s_wd = [mks(f"s_wd{i}") for i in range(2)]
    s_wg = [mks(f"s_wg{i}") for i in range(3)]
    s_wn = [mks(f"s_wn{i}") for i in range(3)]
    s_xr = [mks(f"s_xr{i}") for i in range(2)]
    s_xb = [mks(f"s_xb{i}") for i in range(3)]
    s_xc = mks("s_xc")
    s_out = [mks(f"s_out{i}") for i in range(2)]

    def sb(name, shape, dt):
        return st.enter_context(nc.sbuf_tensor(name, shape, dt))

    c_mask = sb("c_mask", [128, 128], BF16)
    c_id = sb("c_id", [128, 128], BF16)
    c_sg = sb("c_sg", [128, HC], F32)
    c_su = sb("c_su", [128, HC], F32)
    K_fm = sb("K_fm", [128, GROUPS * S], BF16)
    V_tm = sb("V_tm", [128, GROUPS * 16 * 128], BF16)
    Q_bf = sb("Q_bf", [128, HEADS * TILE], BF16)
    act_fm = sb("act_fm", [128, 16 * TILE], BF16)
    xbf = [sb(f"xbf{i}", [128, D], BF16) for i in range(3)]
    xres = [sb(f"xres{i}", [128, D], F32) for i in range(2)]
    x1b = sb("x1b", [128, 4 * D], BF16)
    gu = sb("gu", [128, HC * TILE], BF16)
    sdb_bf = sb("sdb_bf", [128, D], BF16)
    sums = sb("sums", [128, HEADS], F32)
    invs = sb("invs", [128, HEADS], F32)
    bnb = sb("bnb", [128, 24], F32)
    mv01 = sb("mv01", [128, 2], F32)
    negm = sb("negm", [128, 1], F32)
    lntmp = sb("lntmp", [128, 1], F32)
    rstd = sb("rstd", [128, 1], F32)
    epsv = sb("epsv", [128, 1], F32)
    shiftv = sb("shiftv", [128, 1], F32)
    wqb = [sb(f"wqb{i}", [128, 1024], BF16) for i in range(2)]
    wkvb = [sb(f"wkvb{i}", [128, 1024], BF16) for i in range(2)]
    wdb = [sb(f"wdb{i}", [128, 2048], BF16) for i in range(2)]
    wnb = [sb(f"wnb{i}", [128, D], BF16) for i in range(3)]
    m128 = sb("m128", [128, D // 2], BF16)

    s_cst.inc(SY.dma_start(c_mask[:, :], mask_in[:, :]), 16)
    s_cst.inc(SY.dma_start(c_id[:, :], id_in[:, :]), 16)
    s_cst.inc(SY.dma_start(c_sg[:, :], sg_in[:, :]), 16)
    s_cst.inc(SY.dma_start(c_su[:, :], su_in[:, :]), 16)
    CST_ALL = s_cst.n
    s_dv.inc(DV.memset(epsv[:, :], EPS))
    s_dv.inc(DV.memset(shiftv[:, :], -16.0))
    m128_tok = s_dv.inc(DV.memset(m128[:, :], -128.0))

    for i in range(4):
        s_p0a.inc(GP.dma_start(wq_b[i * 512:(i + 1) * 512, :],
                               wq_f[i * 512:(i + 1) * 512, :]), 16)
    P0_WQ = s_p0a.n
    for i in range(4):
        s_p0a.inc(GP.dma_start(wkv_b[i * 512:(i + 1) * 512, :],
                               wkv_f[i * 512:(i + 1) * 512, :]), 16)
    P0_WKV = s_p0a.n
    for i in range(4):
        s_p0a.inc(GP.dma_start(wd_b[i * 512:(i + 1) * 512, :],
                               wd_f[i * 512:(i + 1) * 512, :]), 16)
    P0_WD = s_p0a.n

    xbf_need = []

    def load_xbf(gchunk):
        trow0 = (gchunk // 4) * TILE + (gchunk % 4) * 128
        slot = gchunk % 3
        if gchunk >= 3:
            W.wait(GP, s_xc, gchunk - 2)
        GP.dma_start(xbf[slot][:, :], x_in[trow0: trow0 + 128, :]
                     ).then_inc(s_xb[slot].h, 16)
        s_xb[slot].n += 16
        xbf_need.append((slot, s_xb[slot].n))

    for _c in range(3):
        load_xbf(_c)

    s_cst.inc(GP.dma_start(sdb_bf[:, :], sdb_in[:, :]), 16)
    CST_SDB = s_cst.n
    n_chunks = n_tiles * 4
    for _c in range(3, min(6, n_chunks)):
        load_xbf(_c)
    next_xbf = [min(6, n_chunks)]

    s_pf = mks("s_pf")
    s_acc = mks("s_acc")
    W.wait(GP, s_dv, m128_tok)
    for hc in range(HC):
        for hh in range(4):
            GP.dma_start(wgu_b[hc * 128:(hc + 1) * 128,
                               hh * 1024:(hh + 1) * 1024],
                         m128[:, :]).then_inc(s_pf.h, 16)
            s_pf.n += 16
    for eq in range(4):
        for hcg in range(11):
            for hh in range(2):
                GP.dma_start(wn_b[eq, hcg, :, hh * 1024:(hh + 1) * 1024],
                             m128[:, :]).then_inc(s_pf.h, 16)
                s_pf.n += 16
    W.wait(GP, s_pf, s_pf.n)
    for hc in range(HC):
        GP.dma_start(wgu_b[hc * 128:(hc + 1) * 128, 0:D],
                     wg_i[hc * 128:(hc + 1) * 128, :],
                     accum_op=OP.add).then_inc(s_acc.h, 16)
        s_acc.n += 16
        GP.dma_start(wgu_b[hc * 128:(hc + 1) * 128, D:2 * D],
                     wu_i[hc * 128:(hc + 1) * 128, :],
                     accum_op=OP.add).then_inc(s_acc.h, 16)
        s_acc.n += 16
    ACC_WGU = s_acc.n
    for eq in range(4):
        for hcg in range(11):
            GP.dma_start(
                wn_b[eq, hcg, :, :],
                wn_i[:, :].rearrange("(hc p) e -> p hc e", p=128)[
                    :, hcg * 4:(hcg + 1) * 4, eq * 512:(eq + 1) * 512],
                accum_op=OP.add).then_inc(s_acc.h, 16)
            s_acc.n += 16
    ACC_ALL = s_acc.n

    wq_ld = [0]; wkv_ld = [0]; wd_ld = [0]; wn_ld = [0]; xr_ld = [0]
    wq_cons = [0, 0]; wkv_cons = [0, 0]; wd_cons = [0, 0]; wn_cons = [0, 0, 0]
    xr_cons = [0, 0]
    out_st = [0, 0]
    psum_id = [0]

    def psname(tag):
        psum_id[0] += 1
        return f"ps_{tag}_{psum_id[0]}"

    def stream_load(sem_list, cons, ld_cnt, buf_list, src_ap_fn, gate_extra=None):
        sl = ld_cnt[0] % 2
        W.wait(SY, s_pe, cons[sl])
        if gate_extra is not None:
            gate_extra(sl)
        SY.dma_start(buf_list[sl][:, :], src_ap_fn()).then_inc(sem_list[sl].h, 16)
        sem_list[sl].n += 16
        ld_cnt[0] += 1
        return sl, sem_list[sl].n

    dbg = dbg or []
    dbg_bufs = {"K_fm": K_fm, "V_tm": V_tm, "Q_bf": Q_bf, "act_fm": act_fm,
                "x1b": x1b, "gu": gu}
    ln2_stt_last = [0]
    last_ctx_pe = [0]
    ln_apply_tok = [0]
    prev_dv_end = [0]

    for t in range(n_tiles):
        tt = t % NT_SEQ
        trow = t * TILE

        while next_xbf[0] < min(4 * t + 6, n_chunks):
            load_xbf(next_xbf[0])
            next_xbf[0] += 1
        with ExitStack() as ps:
            ps.enter_context(nc.named_scope(f"xT{t}", notify=True))
            pxt = [ps.enter_context(nc.psum_tensor(psname("xt"), [128, 1024], BF16))
                   for _ in range(2)]
            W.wait(PE, s_dv, prev_dv_end[0])
            evac_tok = {}
            for c in range(4):
                slot, semval = xbf_need[t * 4 + c]
                W.wait(PE, s_xb[slot], semval)
                if t == 0 and c == 0:
                    W.wait(PE, s_cst, CST_ALL)
                for g4 in range(4):
                    bank = (c * 4 + g4) % 2
                    if (c * 4 + g4) >= 2:
                        W.wait(PE, s_dv, evac_tok.get(c * 4 + g4 - 2, 0))
                    for j in range(4):
                        hc = g4 * 4 + j
                        ins = PE.transpose(pxt[bank][:, j * 128:(j + 1) * 128],
                                           xbf[slot][:, hc * 128:(hc + 1) * 128],
                                           c_id[:, :])
                    pe_tok = s_pe.inc(ins)
                    W.wait(DV, s_pe, pe_tok)
                    dst = act_fm[:, :].rearrange("p (hc t) -> p hc t", hc=16)[
                        :, g4 * 4:(g4 + 1) * 4, c * 128:(c + 1) * 128]
                    evac_tok[c * 4 + g4] = s_dv.inc(DV.tensor_copy(dst, pxt[bank][:, 0:512]))
                nc.tensor.nop().then_inc(s_xc.h, 1)
                s_xc.n += 1
            xfm_tok = s_dv.n

        with ExitStack() as ps:
            ps.enter_context(nc.named_scope(f"qkv{t}", notify=True))
            pq = [ps.enter_context(nc.psum_tensor(psname("q"), [128, 512], F32))
                  for _ in range(8)]
            W.wait(PE, s_dv, xfm_tok)
            qev_tok = 0
            for half in range(2):
                if half == 1:
                    W.wait(PE, s_ac, qev_tok)
                for hc in range(16):
                    def gate(sl):
                        if t == 0 and wq_ld[0] < 2:
                            W.wait(SY, s_p0a, P0_WQ)
                    sl, need = stream_load(
                        s_wq, wq_cons, wq_ld, wqb,
                        lambda: wq_b[hc * 128:(hc + 1) * 128,
                                     half * 1024:(half + 1) * 1024], gate)
                    W.wait(PE, s_wq[sl], need)
                    for ec in range(8):
                        ins = PE.matmul(pq[ec][:, :],
                                        wqb[sl][:, ec * 128:(ec + 1) * 128],
                                        act_fm[:, hc * 512:(hc + 1) * 512],
                                        start=(hc == 0), stop=(hc == 15))
                    wq_cons[sl] = s_pe.inc(ins)
                mm_tok = s_pe.n
                W.wait(AC, s_pe, mm_tok)
                for ec in range(8):
                    h_abs = half * 8 + ec
                    ins = AC.activation(
                        Q_bf[:, :].rearrange("p (h t) -> p h t", h=16)[:, h_abs, :],
                        pq[ec][:, :], AF.Copy, bias=0.0, scale=SCALE)
                qev_tok = s_ac.inc(ins)
            W.wait(PE, s_ac, qev_tok)
            for hc in range(16):
                def gate(sl):
                    if t == 0 and wkv_ld[0] < 2:
                        W.wait(SY, s_p0a, P0_WKV)
                sl, need = stream_load(
                    s_wkv, wkv_cons, wkv_ld, wkvb,
                    lambda: wkv_b[hc * 128:(hc + 1) * 128, :], gate)
                W.wait(PE, s_wkv[sl], need)
                for dc in range(4):
                    ins = PE.matmul(pq[dc][:, :], wkvb[sl][:, dc * 128:(dc + 1) * 128],
                                    act_fm[:, hc * 512:(hc + 1) * 512],
                                    start=(hc == 0), stop=(hc == 15))
                for tc in range(4):
                    ins = PE.matmul(pq[4 + tc][:, :],
                                    act_fm[:, hc * 512 + tc * 128: hc * 512 + (tc + 1) * 128],
                                    wkvb[sl][:, 512:1024],
                                    start=(hc == 0), stop=(hc == 15))
                wkv_cons[sl] = s_pe.inc(ins)
            kv_tok = s_pe.n
            W.wait(DV, s_pe, kv_tok)
            for dc in range(4):
                ins = DV.tensor_copy(K_fm[:, dc * S + tt * 512: dc * S + (tt + 1) * 512],
                                     pq[dc][:, :])
            for tc in range(4):
                kc = tt * 4 + tc
                dst = V_tm[:, :].rearrange("p (g kc d) -> p g kc d", g=4, kc=16)[
                    :, :, kc, :]
                ins = DV.tensor_copy(dst, pq[4 + tc][:, :].rearrange(
                    "p (g d) -> p g d", g=4))
            kvev_tok = s_dv.inc(ins)

        stt_prev = [0]
        for qp in range(2):
            with ExitStack() as pairsc:
                pairsc.enter_context(nc.named_scope(f"attn{t}_{qp}", notify=True))
                ctx2 = [pairsc.enter_context(
                    nc.sbuf_tensor(psname("ctx"), [128, 2048], BF16)) for _ in range(2)]
                ctx_done = 0
                exp_all = 0
                for qq in range(2):
                    qc = qp * 2 + qq
                    qcg = tt * 4 + qc
                    klen = (qcg + 1) * 128
                    nkc = qcg + 1
                    ctx = ctx2[qq]
                    with ExitStack() as qs:
                        probs = [qs.enter_context(nc.sbuf_tensor(
                            psname("probs"), [128, 2048], BF16)) for _ in range(2)]
                        probsT = qs.enter_context(nc.sbuf_tensor(
                            psname("probsT"), [128, 2048], BF16))
                        ps_s = qs.enter_context(nc.psum_tensor(psname("s"), [128, 2048], F32))
                        ps_pt = [qs.enter_context(nc.psum_tensor(
                            psname("pt"), [128, 1024], BF16)) for _ in range(2)]
                        ps_cx = [qs.enter_context(nc.psum_tensor(
                            psname("cx"), [128, 512], F32)) for _ in range(2)]

                        W.wait(PE, s_dv, kvev_tok)
                        W.wait(PE, s_dv, stt_prev[0])
                        W.wait(PE, s_ac, s_ac.n)
                        exp_tok = {}; ptev_tok = {}; ctx_tok = {}; ctxev_tok = {}
                        norm_tok = {}
                        for it in range(HEADS + 1):
                            if it < HEADS:
                                h = it
                                g = h // 4
                                if h > 0:
                                    W.wait(PE, s_ac, exp_tok[h - 1])
                                lhsQ = Q_bf[:, h * 512 + qc * 128: h * 512 + (qc + 1) * 128]
                                base = klen - 128
                                blo = (base // 512) * 512
                                if base % 512 != 0:
                                    PE.matmul(ps_s[:, blo:base], lhsQ,
                                              K_fm[:, g * S + blo: g * S + base],
                                              start=True, stop=True)
                                ins = PE.matmul(ps_s[:, base:klen], lhsQ,
                                                K_fm[:, g * S + base: g * S + klen],
                                                start=True, stop=True)
                                scd_tok = s_pe.inc(ins)
                                for j in range(blo // 512):
                                    ins = PE.matmul(
                                        ps_s[:, j * 512:(j + 1) * 512], lhsQ,
                                        K_fm[:, g * S + j * 512: g * S + (j + 1) * 512],
                                        start=True, stop=True)
                                sc_tok = s_pe.inc(ins) if blo > 0 else scd_tok
                                W.wait(DV, s_pe, scd_tok)
                                mtok = s_dv.inc(DV.tensor_tensor(
                                    ps_s[:, klen - 128:klen], ps_s[:, klen - 128:klen],
                                    c_mask[:, :], OP.add))
                                sc_mm = (sc_tok, mtok)
                            if it >= 1:
                                h = it - 1
                                g = h // 4
                                W.wait(PE, s_dv, norm_tok[h])
                                nbat = (nkc + 3) // 4
                                for b in range(nbat):
                                    if b >= 2:
                                        W.wait(PE, s_dv, ptev_tok[(h, b - 2)])
                                    kcs = list(range(b * 4, min(nkc, b * 4 + 4)))
                                    for kk in kcs:
                                        ins = PE.transpose(
                                            ps_pt[b % 2][:, (kk % 4) * 128:(kk % 4 + 1) * 128],
                                            probs[h % 2][:, kk * 128:(kk + 1) * 128],
                                            c_id[:, :])
                                    ptok = s_pe.inc(ins)
                                    W.wait(DV, s_pe, ptok)
                                    if b == 0:
                                        W.wait(DV, s_pe, ctx_tok[h - 1] if h > 0
                                               else last_ctx_pe[0])
                                    wdt = 128 * len(kcs)
                                    ptev_tok[(h, b)] = s_dv.inc(DV.tensor_copy(
                                        probsT[:, b * 512: b * 512 + wdt],
                                        ps_pt[b % 2][:, 0:wdt]))
                                if h % 4 == 0:
                                    W.wait(PE, s_dv, ctxev_tok.get(g - 2, 0))
                                for kk in range(nkc):
                                    W.wait(PE, s_dv, ptev_tok[(h, kk // 4)])
                                    ins = PE.matmul(
                                        ps_cx[g % 2][:, (h % 4) * 128:(h % 4 + 1) * 128],
                                        V_tm[:, (g * 16 + kk) * 128:(g * 16 + kk + 1) * 128],
                                        probsT[:, kk * 128:(kk + 1) * 128],
                                        start=(kk == 0), stop=(kk == nkc - 1))
                                ctx_tok[h] = s_pe.inc(ins)
                                last_ctx_pe[0] = ctx_tok[h]
                                if h % 4 == 3:
                                    W.wait(DV, s_pe, ctx_tok[h])
                                    ctxev_tok[g] = s_dv.inc(DV.tensor_copy(
                                        ctx[:, g * 512:(g + 1) * 512], ps_cx[g % 2][:, :]))
                            if it < HEADS:
                                h = it
                                sc_tok, mtok = sc_mm
                                W.wait(AC, s_dv, mtok)
                                W.wait(AC, s_pe, sc_tok)
                                exp_tok[h] = s_ac.inc(AC.activation(
                                    probs[h % 2][:, 0:klen], ps_s[:, 0:klen], AF.Exp,
                                    bias=shiftv[:, 0:1], scale=1.0,
                                    accum_out=sums[:, h:h + 1]))
                                W.wait(DV, s_ac, exp_tok[h])
                                rc_tok = s_dv.inc(DV.reciprocal(invs[:, h:h + 1],
                                                                sums[:, h:h + 1]))
                                W.wait(DV, s_dv, rc_tok)
                                norm_tok[h] = s_dv.inc(DV.tensor_scalar(
                                    probs[h % 2][:, 0:klen], probs[h % 2][:, 0:klen],
                                    invs[:, h:h + 1], None, OP.mult))
                        ctx_done = s_dv.n
                        exp_all = s_ac.n
                        if dbg and "taps" in dbg and t == 0:
                            dt_ = nc.dram_tensor(f"dbg_ctx_{qc}", [128, 2048], BF16,
                                                 kind="ExternalOutput")
                            dt2_ = nc.dram_tensor(f"dbg_probsT_{qc}", [128, 2048], BF16,
                                                  kind="ExternalOutput")
                            dt3_ = nc.dram_tensor(f"dbg_sums_{qc}", [128, HEADS], F32,
                                                  kind="ExternalOutput")
                            W.wait(SY, s_dv, ctx_done)
                            SY.dma_start(dt_[:, :], ctx[:, :]).then_inc(s_cst.h, 16)
                            SY.dma_start(dt2_[:, :], probsT[:, :]).then_inc(s_cst.h, 16)
                            SY.dma_start(dt3_[:, :], sums[:, :]).then_inc(s_cst.h, 16)
                            s_cst.n += 48
                            W.wait(SY, s_cst, s_cst.n)

                with ExitStack() as ps2:
                    ps2.enter_context(nc.named_scope(f"dense{t}_{qp}", notify=True))
                    ps_do = [ps2.enter_context(nc.psum_tensor(psname("do"), [128, 2048], F32))
                             for _ in range(2)]
                    xr_need = [0, 0]
                    for qq in range(2):
                        qc = qp * 2 + qq
                        sl = qc % 2
                        W.wait(SY, s_dv, xr_cons[sl])
                        W.wait(SY, s_out[sl], out_st[sl])
                        SY.dma_start(xres[sl][:, :],
                                     x_in[trow + qc * 128: trow + (qc + 1) * 128, :]
                                     ).then_inc(s_xr[sl].h, 16)
                        s_xr[sl].n += 16
                        xr_need[qq] = s_xr[sl].n
                        xr_ld[0] += 1
                    W.wait(PE, s_dv, ctx_done)
                    W.wait(PE, s_ac, exp_all)
                    for dc in range(16):
                        def gate(wsl):
                            if t == 0 and wd_ld[0] < 2:
                                W.wait(SY, s_p0a, P0_WD)
                        wsl, need = stream_load(
                            s_wd, wd_cons, wd_ld, wdb,
                            lambda: wd_b[dc * 128:(dc + 1) * 128, :], gate)
                        W.wait(PE, s_wd[wsl], need)
                        for qq in range(2):
                            for n in range(4):
                                ins = PE.matmul(
                                    ps_do[qq][:, n * 512:(n + 1) * 512],
                                    ctx2[qq][:, dc * 128:(dc + 1) * 128],
                                    wdb[wsl][:, n * 512:(n + 1) * 512],
                                    start=(dc == 0), stop=(dc == 15))
                        wd_cons[wsl] = s_pe.inc(ins)
                    do_tok = s_pe.n
                    for qq in range(2):
                        qc = qp * 2 + qq
                        sl = qc % 2
                        W.wait(DV, s_pe, do_tok)
                        W.wait(DV, s_xr[sl], xr_need[qq])
                        stt_tok = s_dv.inc(DV.scalar_tensor_tensor(
                            xres[sl][:, :], ps_do[qq][:, :], 1.0, xres[sl][:, :],
                            OP.mult, OP.add))
                        xr_cons[sl] = stt_tok
                        stt_prev[0] = stt_tok
                        if dbg and "taps" in dbg and t == 0:
                            dt_ = nc.dram_tensor(f"dbg_ln1pre_{qc}", [128, D], F32,
                                                 kind="ExternalOutput")
                            W.wait(SY, s_dv, stt_tok)
                            SY.dma_start(dt_[:, :], xres[(qp*2+qq) % 2][:, :]).then_inc(s_cst.h, 16)
                            s_cst.n += 16
                            W.wait(DV, s_cst, s_cst.n)
                        for j in range(4):
                            ins = DV.bn_stats(bnb[:, j * 6:(j + 1) * 6],
                                              xres[sl][:, j * 512:(j + 1) * 512])
                        btok = s_dv.inc(ins)
                        W.wait(DV, s_dv, btok)
                        atok = s_dv.inc(DV.bn_aggr(
                            mv01[:, :], bnb[:, :].rearrange("p (n t) -> p n t", t=3)))
                        W.wait(DV, s_dv, atok)
                        ngtok = s_dv.inc(DV.tensor_scalar(
                            negm[:, :], mv01[:, 0:1], -1.0, None, OP.mult))
                        W.wait(AC, s_dv, atok)
                        lntok = s_ac.inc(AC.activation(
                            lntmp[:, :], mv01[:, 1:2], AF.Ln, bias=epsv[:, 0:1], scale=1.0))
                        W.wait(AC, s_ac, lntok)
                        W.wait(AC, s_dv, ln_apply_tok[0])
                        rstok = s_ac.inc(AC.activation(
                            rstd[:, :], lntmp[:, :], AF.Exp, bias=0.0, scale=-0.5))
                        W.wait(DV, s_ac, rstok)
                        W.wait(DV, s_dv, ngtok)
                        ln_apply_tok[0] = s_dv.inc(DV.tensor_scalar(
                            x1b[:, qc * D:(qc + 1) * D], xres[sl][:, :],
                            negm[:, 0:1], rstd[:, 0:1], OP.add, OP.mult))
                        xr_cons[sl] = ln_apply_tok[0]
        ln1_done = s_dv.n

        with ExitStack() as ps:
            ps.enter_context(nc.named_scope(f"x1T{t}", notify=True))
            pxt = [ps.enter_context(nc.psum_tensor(psname("x1t"), [128, 1024], BF16))
                   for _ in range(2)]
            W.wait(PE, s_dv, ln1_done)
            evac_tok = {}
            for qc in range(4):
                for g4 in range(4):
                    bank = (qc * 4 + g4) % 2
                    if (qc * 4 + g4) >= 2:
                        W.wait(PE, s_dv, evac_tok.get(qc * 4 + g4 - 2, 0))
                    for j in range(4):
                        dc = g4 * 4 + j
                        ins = PE.transpose(pxt[bank][:, j * 128:(j + 1) * 128],
                                           x1b[:, qc * D + dc * 128: qc * D + (dc + 1) * 128],
                                           c_id[:, :])
                    pe_tok = s_pe.inc(ins)
                    W.wait(DV, s_pe, pe_tok)
                    dst = act_fm[:, :].rearrange("p (hc t) -> p hc t", hc=16)[
                        :, g4 * 4:(g4 + 1) * 4, qc * 128:(qc + 1) * 128]
                    evac_tok[qc * 4 + g4] = s_dv.inc(DV.tensor_copy(dst, pxt[bank][:, 0:512]))
            x1fm_tok = s_dv.n

        with ExitStack() as ps:
            ps.enter_context(nc.named_scope(f"gate{t}", notify=True))
            pg = [ps.enter_context(nc.psum_tensor(psname("g"), [128, 512], F32))
                  for _ in range(2)]
            pu = [ps.enter_context(nc.psum_tensor(psname("u"), [128, 512], F32))
                  for _ in range(2)]
            wgub = [ps.enter_context(nc.sbuf_tensor(psname("wgub"), [128, 2 * D], BF16))
                    for _ in range(2)]
            gsil = [ps.enter_context(nc.sbuf_tensor(psname("gsil"), [128, TILE], BF16))
                    for _ in range(2)]
            wgu_cons = [s_pe.n] * 2
            W.wait(SY, s_dv, s_dv.n)
            W.wait(PE, s_dv, x1fm_tok)
            sil_tok = {}; gu_tok = {}
            for hc in range(HC):
                sl = hc % 2
                W.wait(SY, s_pe, wgu_cons[sl])
                if t == 0:
                    W.wait(SY, s_acc, ACC_WGU)
                SY.dma_start(wgub[sl][:, :], wgu_b[hc * 128:(hc + 1) * 128, :]
                             ).then_inc(s_wg[sl].h, 16)
                s_wg[sl].n += 16
                need = s_wg[sl].n
                W.wait(PE, s_wg[sl], need)
                if hc >= 2:
                    W.wait(PE, s_ac, sil_tok[hc - 2])
                    W.wait(PE, s_dv, gu_tok[hc - 2])
                psl = hc % 2
                for dc in range(16):
                    ins = PE.matmul(pg[psl][:, :], wgub[sl][:, dc * 128:(dc + 1) * 128],
                                    act_fm[:, dc * 512:(dc + 1) * 512],
                                    start=(dc == 0), stop=(dc == 15))
                g_tok = s_pe.inc(ins)
                for dc in range(16):
                    ins = PE.matmul(pu[psl][:, :],
                                    wgub[sl][:, D + dc * 128: D + (dc + 1) * 128],
                                    act_fm[:, dc * 512:(dc + 1) * 512],
                                    start=(dc == 0), stop=(dc == 15))
                u_tok = s_pe.inc(ins)
                wgu_cons[sl] = u_tok
                W.wait(AC, s_pe, g_tok)
                sil_tok[hc] = s_ac.inc(AC.activation(
                    gsil[psl][:, :], pg[psl][:, :], AF.Silu,
                    bias=0.0, scale=c_sg[:, hc:hc + 1]))
                W.wait(DV, s_pe, u_tok)
                W.wait(DV, s_ac, sil_tok[hc])
                gu_tok[hc] = s_dv.inc(DV.scalar_tensor_tensor(
                    gu[:, hc * 512:(hc + 1) * 512], pu[psl][:, :],
                    c_su[:, hc:hc + 1], gsil[psl][:, :], OP.mult, OP.mult))
            gu_done = s_dv.n

        with ExitStack() as ps:
            ps.enter_context(nc.named_scope(f"down{t}", notify=True))
            pdn2 = [[ps.enter_context(nc.psum_tensor(psname("dn"), [128, 512], F32))
                     for _ in range(4)] for _ in range(2)]
            y2 = ps.enter_context(nc.sbuf_tensor(psname("y2"), [128, 4 * D], BF16))
            bnb4 = ps.enter_context(nc.sbuf_tensor(psname("bnb4"), [128, 96], F32))
            W.wait(PE, s_dv, gu_done)
            ev_tok = {}
            for eq in range(4):
                for hcg in range(11):
                    wsl = wn_ld[0] % 3
                    W.wait(SY, s_pe, wn_cons[wsl])
                    if t == 0:
                        W.wait(SY, s_acc, ACC_ALL)
                    SY.dma_start(wnb[wsl][:, :], wn_b[eq, hcg, :, :]
                                 ).then_inc(s_wn[wsl].h, 16)
                    s_wn[wsl].n += 16
                    need = s_wn[wsl].n
                    wn_ld[0] += 1
                    W.wait(PE, s_wn[wsl], need)
                    if hcg == 0 and eq > 1:
                        W.wait(PE, s_dv, ev_tok[eq - 2])
                    for j in range(4):
                        hc = hcg * 4 + j
                        for qc in range(4):
                            ins = PE.matmul(
                                pdn2[eq % 2][qc][:, :],
                                gu[:, hc * 512 + qc * 128: hc * 512 + (qc + 1) * 128],
                                wnb[wsl][:, j * 512:(j + 1) * 512],
                                start=(hc == 0), stop=(hc == HC - 1))
                    wn_cons[wsl] = s_pe.inc(ins)
                dn_tok = s_pe.n
                W.wait(DV, s_pe, dn_tok)
                W.wait(DV, s_cst, CST_SDB)
                for qc in range(4):
                    sli = slice(qc * D + eq * 512, qc * D + (eq + 1) * 512)
                    s_dv.inc(DV.tensor_tensor(
                        y2[:, sli], pdn2[eq % 2][qc][:, :],
                        sdb_bf[:, eq * 512:(eq + 1) * 512], OP.mult))
                    s_dv.inc(DV.tensor_tensor(y2[:, sli], y2[:, sli],
                                              x1b[:, sli], OP.add))
                    ins = DV.bn_stats(bnb4[:, qc * 24 + eq * 6: qc * 24 + (eq + 1) * 6],
                                      y2[:, sli])
                ev_tok[eq] = s_dv.inc(ins)
            for qc in range(4):
                sl = qc % 2
                W.wait(DV, s_dv, ev_tok[3])
                atok = s_dv.inc(DV.bn_aggr(
                    mv01[:, :],
                    bnb4[:, qc * 24:(qc + 1) * 24].rearrange("p (n t) -> p n t", t=3)))
                W.wait(DV, s_dv, atok)
                ngtok = s_dv.inc(DV.tensor_scalar(
                    negm[:, :], mv01[:, 0:1], -1.0, None, OP.mult))
                W.wait(AC, s_dv, atok)
                lntok = s_ac.inc(AC.activation(
                    lntmp[:, :], mv01[:, 1:2], AF.Ln, bias=epsv[:, 0:1], scale=1.0))
                W.wait(AC, s_ac, lntok)
                W.wait(AC, s_dv, ln_apply_tok[0])
                rstok = s_ac.inc(AC.activation(
                    rstd[:, :], lntmp[:, :], AF.Exp, bias=0.0, scale=-0.5))
                W.wait(DV, s_ac, rstok)
                W.wait(DV, s_dv, ngtok)
                W.wait(DV, s_out[sl], out_st[sl])
                W.wait(DV, s_xr[sl], s_xr[sl].n)
                ap_tok = s_dv.inc(DV.tensor_scalar(
                    xres[sl][:, :], y2[:, qc * D:(qc + 1) * D],
                    negm[:, 0:1], rstd[:, 0:1], OP.add, OP.mult))
                ln_apply_tok[0] = ap_tok
                xr_cons[sl] = ap_tok
                W.wait(SY, s_dv, ap_tok)
                SY.dma_start(out[trow + qc * 128: trow + (qc + 1) * 128, :],
                             xres[sl][:, :]).then_inc(s_out[sl].h, 16)
                s_out[sl].n += 16
                out_st[sl] = s_out[sl].n
        prev_dv_end[0] = s_dv.n

    W.wait(SY, s_out[0], s_out[0].n)
    W.wait(SY, s_out[1], s_out[1].n)

    if dbg:
        dbg_sem = Sem(st, nc, "s_dbg")
        W.wait(SY, s_dv, s_dv.n)
        W.wait(SY, s_ac, s_ac.n)
        for name in dbg:
            buf = dbg_bufs[name]
            shp = [128, buf.shape[1]]
            dt_ = nc.dram_tensor(f"dbg_{name}", shp, BF16, kind="ExternalOutput")
            dbg_sem.inc(SY.dma_start(dt_[:, :], buf[:, :]), 16)
        SY.wait_ge(dbg_sem.h, dbg_sem.n)

    st.close()
    return nc


def prep_inputs(L0_attn_x, L0_attn_w_q, L0_attn_w_kv, L0_attn_w_dense,
                L0_ffn_wq_gate, L0_ffn_s_gate, L0_ffn_wq_up, L0_ffn_s_up,
                L0_ffn_wq_down, L0_ffn_s_down):
    x = np.ascontiguousarray(np.asarray(L0_attn_x, dtype=np.float32))
    wq = np.ascontiguousarray(np.asarray(L0_attn_w_q, dtype=np.float32))
    wkv = np.ascontiguousarray(np.asarray(L0_attn_w_kv, dtype=np.float32))
    wd = np.ascontiguousarray(np.asarray(L0_attn_w_dense, dtype=np.float32))
    wg = np.asarray(L0_ffn_wq_gate, dtype=np.int32)
    wu = np.asarray(L0_ffn_wq_up, dtype=np.int32)
    wn = np.asarray(L0_ffn_wq_down, dtype=np.int32)
    sg = np.asarray(L0_ffn_s_gate, dtype=np.float32)
    su = np.asarray(L0_ffn_s_up, dtype=np.float32)
    sd = np.asarray(L0_ffn_s_down, dtype=np.float32)

    def perm_gate(w):
        return np.ascontiguousarray(
            w.reshape(HC, 128, 16, 128).transpose(0, 3, 2, 1)).reshape(HID, D)
    wg_p = perm_gate(wg)
    wu_p = perm_gate(wu)
    wn_p = np.ascontiguousarray(
        wn.reshape(D, HC, 128).transpose(1, 2, 0)).reshape(HID, D)
    sg_t = np.ascontiguousarray(sg.reshape(HC, 128).T)
    su_t = np.ascontiguousarray(su.reshape(HC, 128).T)
    sdb = np.ascontiguousarray(np.tile(sd.reshape(1, D), (128, 1)))

    mask = np.where(np.arange(128)[:, None] >= np.arange(128)[None, :],
                    np.float32(0.0), np.float32(NEGBIG)).astype(ml_dtypes.bfloat16)
    ident = np.eye(128, dtype=ml_dtypes.bfloat16)

    shared = {"wq_f": wq, "wkv_f": wkv, "wd_f": wd,
              "wg_i": wg_p, "wu_i": wu_p, "wn_i": wn_p,
              "sg_in": sg_t, "su_in": su_t, "sdb_in": sdb,
              "mask_in": mask, "id_in": ident}
    in_maps = []
    for c in range(NCORES):
        m = dict(shared)
        m["x_in"] = np.ascontiguousarray(
            x[c * SEQPC:(c + 1) * SEQPC].reshape(SEQPC * S, D))
        in_maps.append(m)
    return in_maps


def kernel(**inputs):
    in_maps = prep_inputs(**inputs)
    nc = build_program()
    res = run_bass_kernel_spmd(nc, in_maps, core_ids=list(range(NCORES)))
    outs = [r["out"].reshape(SEQPC, S, D) for r in res.results]
    return np.ascontiguousarray(np.concatenate(outs, axis=0).astype(np.float32))
